# version 17
# baseline (speedup 1.0000x reference)
import contextlib
import numpy as np
import ml_dtypes
import concourse.bass as bass
import concourse.mybir as mybir
from concourse.bass_utils import run_bass_kernel_spmd

F32 = mybir.dt.float32
BF16 = mybir.dt.bfloat16
AF = mybir.ActivationFunctionType
OP = mybir.AluOpType
AX = mybir.AxisListType.X

DBG = None
T = 2048
D = 1024
H = 8
NT = 16
ITERS = 20
NS = 2 * ITERS
NWARM = 6


class Run:
    def __init__(self, eng=None, plan=None, sems=None, E=None):
        self.eng = eng
        self.planning = plan is None
        self.plan = {} if plan is None else plan
        self.counts = {}
        self.sems = sems
        self.E = E

    def mark(self, sem, event, inst, n=1):
        if self.planning:
            c = self.counts.get(sem, 0) + n
            self.counts[sem] = c
            assert event not in self.plan, f"dup event {event}"
            self.plan[event] = (sem, c)
        if inst is not None:
            inst.then_inc(self.sems[sem], n)

    def wait(self, eng, event):
        if self.planning or eng != self.eng:
            return
        sem, c = self.plan[event]
        self.E.wait_ge(self.sems[sem], c)

    def own(self, eng):
        return (not self.planning) and eng == self.eng


def body(r, nc, io, bufs):
    t = r.E if r.own('t') else None
    sc = r.E if r.own('s') else None
    v = r.E if r.own('v') else None
    y = r.E if r.own('y') else None
    g = r.E if r.own('g') else None

    A, Bm, Q, M, pzBig, pzB2 = bufs

    Af32 = A.bitcast(F32)
    Bf32 = Bm.bitcast(F32)
    LNx = [A[:, i * 1024:(i + 1) * 1024] for i in range(NT)]
    LNxT = [Bm[:, j * T:(j + 1) * T] for j in range(8)]
    Atile = [A[:, i * T:(i + 1) * T] for i in range(NT)]
    Btile = [Bm[:, i * T:(i + 1) * T] for i in range(NT)]
    atT = [A[:, j * T:(j + 1) * T] for j in range(4)]

    def xt(i):
        if i < 8:
            return Af32[:, 8192 + i * 1024: 8192 + (i + 1) * 1024]
        return Bf32[:, 8192 + (i - 8) * 1024: 8192 + (i - 7) * 1024]

    s2 = Af32[:, 0:1024]
    hb = [Af32[:, 1024:2048], Af32[:, 2048:3072]]
    gA = Af32[:, 3072:4096]
    gB = Af32[:, 4096:5120]
    bout_r = Af32[0:1, 5120:6144]
    hbe_r = Af32[0:1, 6144:7168]
    tmp_r = Af32[0:1, 7168:8192]
    bo_r = Bf32[0:1, 512:1536]
    Woutb0 = Q[:, 8192:9216]

    stg = [Q[:, 0:2048], Q[:, 2048:4096]]
    QTs = [Q[:, 0:2048], Q[:, 2048:4096]]
    KTs = [Q[:, 4096:6144], Q[:, 6144:8192]]
    VTs = Q[:, 8192:10240]
    Vk = Q[:, 10240:11264]
    gV = Q[:, 14336:15360]
    aslc = Q[:, 15360:17408]
    ubc16 = Q[:, 12288:14336]
    urow16 = Q[:, 21504:23552]
    Wqt = [Q[:, 17408 + k * 512:17408 + (k + 1) * 512] for k in range(8)]
    Wkt = [Q[:, 21504 + k * 512:21504 + (k + 1) * 512] for k in range(8)]
    Wvt = [Q[:, 25600 + k * 512:25600 + (k + 1) * 512] for k in range(8)]
    Wint = [Q[:, 29696 + k * 64:29696 + (k + 1) * 64] for k in range(8)]
    ubc32 = Q[:, 17408:21504].bitcast(F32)
    Wot = [Q[:, j * 1024:(j + 1) * 1024] for j in range(4)]
    Woutb2 = Q[:, 4096:5120]
    Winb2 = [Q[:, 5120 + k * 64:5120 + (k + 1) * 64] for k in range(8)]

    yrow = M[:, 0:2048]
    ident = M[:, 2048:2176]
    ones16 = M[:, 2176:2304]
    ucol16 = M[:, 2304:2320]
    gcol16 = M[:, 2320:2336]
    bo16 = M[:, 2336:3360]
    hbe16 = M[:, 3360:4384]
    b2c16 = M[:, 4384:4385]
    Mf = M[:, 4608:9216].bitcast(F32)
    ones32 = Mf[:, 0:128]
    gcol32 = Mf[:, 128:144]
    bnst = Mf[:, 144:156]
    bnagg = Mf[:, 160:192]
    mean = Mf[:, 1152:1168]
    var = Mf[:, 1168:1184]
    rstd = Mf[:, 1184:1200]
    nms = Mf[:, 1200:1216]
    msq = Mf[:, 1216:1232]
    bqc = Mf[:, 1232:1236]
    bkc = Mf[:, 1236:1240]
    bvc = Mf[:, 1240:1244]
    binc = Mf[:, 1244:1245]
    b2c = Mf[:, 1245:1246]
    w1c = Mf[:, 1248:1264]
    b1c = Mf[:, 1264:1280]
    w2h = Mf[:, 1280:1296]
    w2r = Mf[:, 1296:1312]

    pz = [pzBig[:, c * 512:(c + 1) * 512] for c in range(4)]
    pzh = [pzBig[:, 0:1024], pzBig[:, 1024:2048]]
    pt = pzB2[:, 0:16]
    pwarm = pzB2[:, 512:1024]
    pb = [pzB2[:, 1024:1536].bitcast(BF16), pzB2[:, 1536:2048].bitcast(BF16)]
    pzc = [pzB2[:, c * 512:(c + 1) * 512] for c in range(4)]

    (x_d, wq_d, bq_d, wk_d, bk_d, wv_d, bv_d, wo_d, bo_d,
     win_d, bin_d, w1_d, b1_d, w2_d, b2_d, wout_d, bout_d,
     ident_d, out_d, xs_d, qt_d, kt_d, vt_d, at_d) = io

    LP = lambda: nc.allow_low_precision(reason="bf16 matmul-operand staging (blessed)")

    for i in range(NT):
        inst = y.dma_start(out=xt(i), in_=x_d[i * 128:(i + 1) * 128, :]) if y else None
        r.mark('ldx' if i < 8 else 'ldx2', f'x{i}', inst, 16)
    inst = y.dma_start(out=ident[:, :], in_=ident_d[:, :]) if y else None
    r.mark('ldsm', 'ident', inst, 16)
    for nm, dst, src in (('bq', bqc, bq_d), ('bk', bkc, bk_d), ('bv', bvc, bv_d)):
        inst = y.dma_start(out=dst[:, :], in_=src.rearrange("m p -> p m"),
                           allow_slow_non_contiguous=True) if y else None
        r.mark('ldsm', nm, inst, 16)
    inst = y.dma_start(out=binc[:, :], in_=bin_d[:, :]) if y else None
    r.mark('ldsm', 'bin0', inst, 16)
    inst = y.dma_start(out=b2c[0:64, :], in_=b2_d[0:64, :]) if y else None
    r.mark('ldsm', 'b2', inst, 16)
    for nm, dst, src in (('w1', w1c, w1_d), ('b1', b1c, b1_d), ('w2', w2r, w2_d)):
        inst = y.dma_start(out=dst[:, :], in_=src[:, :]) if y else None
        r.mark('ldsm', f'{nm}0', inst, 16)
    inst = y.dma_start(out=bo_r, in_=bo_d.rearrange("(a d) -> a d", a=1)) if y else None
    r.mark('ldsm', 'bo', inst, 16)
    inst = y.dma_start(out=bout_r, in_=bout_d.rearrange("(a d) -> a d", a=1)) if y else None
    r.mark('ldsm', 'bout', inst, 16)
    for k in range(8):
        inst = g.dma_start(out=Wqt[k], in_=wq_d[k * 128:(k + 1) * 128, :]) if g else None
        r.mark('ldw', f'wq{k}', inst, 16)
        inst = g.dma_start(out=Wkt[k], in_=wk_d[k * 128:(k + 1) * 128, :]) if g else None
        r.mark('ldw', f'wk{k}', inst, 16)
        inst = g.dma_start(out=Wvt[k], in_=wv_d[k * 128:(k + 1) * 128, :]) if g else None
        r.mark('ldw', f'wv{k}', inst, 16)
        inst = g.dma_start(out=Wint[k], in_=win_d[k * 128:(k + 1) * 128, :]) if g else None
        r.mark('ldw', f'win{k}', inst, 16)
    inst = g.dma_start(out=Woutb0[0:64, :], in_=wout_d[:, :]) if g else None
    r.mark('ldw', 'wout', inst, 16)
    inst = g.dma_start(out=Woutb0[64:128, :], in_=wout_d[:, :]) if g else None
    r.mark('ldw', 'woutd', inst, 16)

    if v:
        v.memset(ones16[:, :], 1.0)
        v.memset(ones32[:, :], 1.0)
    inst = v.memset(yrow[:, :], 0.0) if v else None
    r.mark('vm', 'consts', inst)
    r.wait('v', 'bout')
    inst = v.tensor_scalar_mul(w2h[:, :], w2r[:, :], 0.5) if v else None
    r.mark('vm', 'w2half', inst)

    r.wait('v', 'bout')
    with LP():
        inst = v.tensor_copy(out=b2c16[0:64, :], in_=b2c[0:64, :]) if v else None
    r.mark('vm', 'b2c16', inst)
    r.wait('t', 'b2c16')
    r.wait('t', 'woutd')
    for c in range(2):
        inst = t.matmul(pzc[c][0:1, :], b2c16[0:64, 0:1], Woutb0[0:64, c * 512:(c + 1) * 512],
                        start=True, stop=True) if t else None
        r.mark('tm', f'wtb2_{c}', inst)
    r.wait('s', 'bout')
    inst = sc.mul(tmp_r, bout_r, 0.5) if sc else None
    r.mark('sm', 'halfbout', inst)
    r.wait('v', 'wtb2_0')
    r.wait('v', 'wtb2_1')
    r.wait('v', 'halfbout')
    for c in range(2):
        with LP():
            inst = v.scalar_tensor_tensor(out=hbe_r[:, c * 512:(c + 1) * 512],
                                          in0=pzc[c][0:1, :], scalar=0.5,
                                          in1=tmp_r[:, c * 512:(c + 1) * 512],
                                          op0=OP.mult, op1=OP.add) if v else None
        r.mark('vm', f'hbe{c}', inst)
    r.wait('s', 'hbe0')
    r.wait('s', 'hbe1')
    with LP():
        inst = sc.copy(out=hbe16[0:1, :], in_=hbe_r) if sc else None
    r.mark('sm', 'hbe16', inst)
    r.wait('s', 'bout')
    with LP():
        inst = sc.copy(out=bo16[0:1, :], in_=bo_r) if sc else None
    r.mark('sm', 'bo16', inst)

    def warm(n):
        if t:
            for w in range(n):
                t.matmul(pwarm[0:1, :], ones16[:, 0:1], yrow[:, (w % 3) * 512:(w % 3) * 512 + 512],
                         start=True, stop=True, tile_position=(0, 0))

    def layernorm(tag, xev, guard=None):
        meanv = bnagg[:, 0:32:2]
        varv = bnagg[:, 1:32:2]
        for i in range(NT):
            if i == 0 and guard is not None:
                r.wait('v', guard)
                r.wait('s', guard)
            r.wait('v', xev(i))
            if i > 0:
                r.wait('v', f'{tag}ag{i - 1}')
            inst = v.bn_stats(bnst[:, 0:6], xt(i)[:, 0:512]) if v else None
            r.mark('vm', f'{tag}ba{i}', inst)
            r.wait('v', f'{tag}ba{i}')
            inst = v.bn_stats(bnst[:, 6:12], xt(i)[:, 512:1024]) if v else None
            r.mark('vm', f'{tag}bb{i}', inst)
            r.wait('v', f'{tag}bb{i}')
            inst = v.bn_aggr(bnagg[:, 2 * i:2 * i + 2], bnst[:, 0:12]) if v else None
            r.mark('vm', f'{tag}ag{i}', inst)
        r.wait('v', f'{tag}ag{NT - 1}')
        inst = v.tensor_scalar_add(var[:, :], varv, 1e-5) if v else None
        r.mark('vm', f'{tag}m5', inst)
        r.wait('s', f'{tag}m5')
        inst = sc.sqrt(msq[:, :], var[:, :]) if sc else None
        r.mark('sm', f'{tag}m6', inst)
        r.wait('v', f'{tag}m6')
        inst = v.reciprocal(rstd[:, :], msq[:, :]) if v else None
        r.mark('vm', f'{tag}m7', inst)
        r.wait('v', f'{tag}m7')
        with LP():
            inst = v.scalar_tensor_tensor(out=nms[:, :], in0=meanv, scalar=-1.0,
                                          in1=rstd[:, :], op0=OP.mult, op1=OP.mult) if v else None
        r.mark('vm', f'{tag}m8', inst)
        for i in range(NT):
            r.wait('s', f'{tag}m8')
            with LP():
                inst = sc.activation(out=LNx[i], in_=xt(i), func=AF.Identity,
                                     scale=rstd[:, i:i + 1], bias=nms[:, i:i + 1]) if sc else None
            r.mark('sm', f'{tag}ln{i}', inst)

    def transpose_lnx(tag):
        r.wait('t', 'bout')
        for j in range(8):
            for half in range(2):
                idx = j * 2 + half
                which = idx % 2
                if idx >= 2:
                    r.wait('t', f'{tag}tpe{idx - 2}')
                for i in range(8 * half, 8 * half + 8):
                    r.wait('t', f'{tag}ln{i}')
                    inst = t.transpose(pb[which][:, (i - 8 * half) * 128:(i - 8 * half + 1) * 128],
                                       LNx[i][:, j * 128:(j + 1) * 128], ident[:, :]) if t else None
                r.mark('tm', f'{tag}tp{idx}', inst)
                r.wait('v', f'{tag}tp{idx}')
                inst = v.tensor_copy(out=LNxT[j][:, half * 1024:(half + 1) * 1024],
                                     in_=pb[which][:, :]) if v else None
                r.mark('vm', f'{tag}tpe{idx}', inst)

    def ffn(tag, win_tiles, woutb, win_ev, wout_ev, pzfree=None):
        for c in range(4):
            r.wait('t', win_ev)
            if c == 0 and pzfree is not None:
                r.wait('t', pzfree)
            for k in range(8):
                r.wait('t', f'{tag}tpe{k * 2 + (c // 2)}')
                inst = t.matmul(pz[c][0:64, :], win_tiles[k][:, :],
                                LNxT[k][:, c * 512:(c + 1) * 512],
                                start=(k == 0), stop=(k == 7)) if t else None
            r.mark('tm', f'{tag}sin{c}', inst)
        for c in range(4):
            r.wait('s', f'{tag}sin{c}')
            r.wait('s', 'bout')
            half, col = c % 2, (c // 2) * 512
            inst = sc.activation(out=s2[64 * half:64 * half + 64, col:col + 512],
                                 in_=pz[c][0:64, :], func=AF.Identity,
                                 bias=binc[64 * half:64 * half + 64, :]) if sc else None
            r.mark('sm', f'{tag}s2_{c}', inst)
        for i in range(16):
            for c in range(4):
                r.wait('s', f'{tag}s2_{c}')
            r.wait('s', 'bout')
            if i >= 2:
                r.wait('s', f'{tag}g{i - 2}')
            inst = sc.activation(out=hb[i % 2], in_=s2[:, :], func=AF.Relu,
                                 scale=w1c[:, i:i + 1], bias=b1c[:, i:i + 1]) if sc else None
            r.mark('sm', f'{tag}h{i}', inst)
            r.wait('v', f'{tag}h{i}')
            gsrc, gdst = (gA, gB) if i % 2 else (gB, gA)
            if i == 0:
                r.wait('v', 'w2half')
                inst = v.tensor_scalar_mul(gdst, hb[i % 2], w2h[:, 0:1]) if v else None
            else:
                r.wait('v', f'{tag}g{i - 1}')
                with LP():
                    inst = v.scalar_tensor_tensor(out=gdst, in0=hb[i % 2], scalar=w2h[:, i:i + 1],
                                                  in1=gsrc, op0=OP.mult, op1=OP.add) if v else None
            r.mark('vm', f'{tag}g{i}', inst)
        gfin = gB
        if DBG == 'S2' and tag == 'L1':
            for c in range(4):
                r.wait('y', f'{tag}s2_{c}')
            inst = y.dma_start(out=out_d[0:128, :], in_=s2) if y else None
            r.mark('lst', 'dbgs2', inst, 16)
            r.wait('v', f'{tag}g15')
            r.wait('y', f'{tag}g15')
            inst = y.dma_start(out=out_d[128:256, :], in_=gfin) if y else None
            r.mark('lst', 'dbgg', inst, 16)
            r.wait('y', 'dbgg')
            return True
        r.wait('v', f'{tag}g15')
        with LP():
            inst = v.tensor_copy(out=stg[0][:, 0:1024], in_=gfin) if v else None
        r.mark('vm', f'{tag}g16', inst)
        gb16 = stg[0][:, 0:1024]
        for i in range(NT):
            half = (i // 4) % 2
            col = (i // 8) * 512 + (i % 4) * 128
            which = i % 2
            if i >= 2:
                r.wait('t', f'{tag}xadd{i - 2}')
            r.wait('t', f'{tag}g16')
            r.wait('t', 'hbe16')
            r.wait('t', 'consts')
            r.wait('t', wout_ev)
            for nch in range(2):
                inst = t.matmul(pzh[which][:, nch * 512:(nch + 1) * 512],
                                gb16[64 * half:64 * half + 64, col:col + 128],
                                woutb[64 * half:64 * half + 64, nch * 512:(nch + 1) * 512],
                                start=True, stop=False) if t else None
                inst = t.matmul(pzh[which][:, nch * 512:(nch + 1) * 512],
                                ones16[0:1, 0:128], hbe16[0:1, nch * 512:(nch + 1) * 512],
                                start=False, stop=True, tile_position=(0, 0)) if t else None
            r.mark('tm', f'{tag}out{i}', inst)
            r.wait('v', f'{tag}out{i}')
            inst = v.tensor_add(xt(i), xt(i), pzh[which][:, :]) if v else None
            r.mark('vm', f'{tag}xadd{i}', inst)

    if DBG == 'WIN':
        for k in range(8):
            r.wait('g', 'woutd')
            inst = g.dma_start(out=out_d[k * 128:(k + 1) * 128, 0:64], in_=Wint[k]) if g else None
            r.mark('lst', f'dbgw{k}', inst, 16)
        r.wait('g', 'dbgw7')
        return

    layernorm('L1', lambda i: 'x7' if i < 8 else 'x15')
    if DBG == 'LN1':
        for i in range(NT):
            r.wait('g', f'L1ln{i}')
            inst = g.dma_start(out=out_d[i * 128:(i + 1) * 128, :], in_=LNx[i]) if g else None
            r.mark('lst', f'dbgln{i}', inst, 16)
        r.wait('g', f'dbgln{NT - 1}')
        return
    transpose_lnx('L1')
    if DBG == 'LNT':
        for j in range(8):
            r.wait('g', f'L1tpe{j * 2}')
            r.wait('g', f'L1tpe{j * 2 + 1}')
            inst = g.dma_start(out=out_d[j * 256:(j + 1) * 256, :].rearrange("(p b) c -> p (b c)", b=2),
                               in_=LNxT[j]) if g else None
            r.mark('lst', f'dbglnt{j}', inst, 16)
        r.wait('g', 'dbglnt7')
        return
    if ffn('L1', Wint, Woutb0, 'woutd', 'woutd'):
        return

    def dbg_dump_x(pref):
        for i in range(NT):
            r.wait('y', f'{pref}{i}')
            inst = y.dma_start(out=out_d[i * 128:(i + 1) * 128, :], in_=xt(i)) if y else None
            r.mark('lst', f'dbgst{i}', inst, 16)
        r.wait('y', f'dbgst{NT - 1}')

    if DBG == 'P1':
        dbg_dump_x('L1xadd')
        return

    def slice_load(h):
        if h == 0:
            r.wait('y', 'P2sp10')
            r.wait('y', 'P2sp11')
        sem = 'lslA' if h % 2 == 0 else 'lslB'
        inst = y.dma_start(out=QTs[h % 2][0:64, :], in_=qt_d[64 * h:64 * h + 64, :]) if y else None
        r.mark(sem, f'sl{h}q', inst, 16)
        inst = y.dma_start(out=KTs[h % 2][0:64, :], in_=kt_d[64 * h:64 * h + 64, :]) if y else None
        r.mark(sem, f'sl{h}k', inst, 16)

    layernorm('L2', lambda i: f'L1xadd{i}')
    transpose_lnx('L2')
    for pi, (wt, bcv, dst_d, bev, wev) in enumerate((
            (Wqt, bqc, qt_d, 'bq', 'wq7'), (Wkt, bkc, kt_d, 'bk', 'wk7'),
            (Wvt, bvc, vt_d, 'bv', 'wv7'))):
        for m in range(4):
            idx = pi * 4 + m
            which = idx % 2
            psrc = pzBig if which == 0 else pzB2
            if idx >= 2:
                r.wait('t', f'P2ev{idx - 2}')
            if idx == 0:
                r.wait('t', 'L1xadd15')
            r.wait('t', 'woutd')
            for c in range(4):
                for k in range(8):
                    r.wait('t', f'L2tpe{k * 2 + (c // 2)}')
                    inst = t.matmul(psrc[:, c * 512:(c + 1) * 512],
                                    wt[k][:, m * 128:(m + 1) * 128],
                                    LNxT[k][:, c * 512:(c + 1) * 512],
                                    start=(k == 0), stop=(k == 7)) if t else None
            r.mark('tm', f'P2mm{idx}', inst)
            r.wait('s', f'P2mm{idx}')
            r.wait('s', 'bout')
            if idx >= 2:
                r.wait('s', f'P2sp{idx - 2}')
            with LP():
                inst = sc.activation(out=stg[which][:, :], in_=psrc[:, :], func=AF.Identity,
                                     bias=bcv[:, m:m + 1]) if sc else None
            r.mark('sm', f'P2ev{idx}', inst)
            r.wait('y', f'P2ev{idx}')
            inst = y.dma_start(out=dst_d[m * 128:(m + 1) * 128, :], in_=stg[which][:, :]) if y else None
            r.mark('lspA' if which == 0 else 'lspB', f'P2sp{idx}', inst, 16)
    slice_load(0)

    for i in range(NT):
        r.wait('y', f'L2ln{i}')
        inst = y.dma_start(out=xs_d[i * 128:(i + 1) * 128, :], in_=xt(i)) if y else None
        r.mark('lxs', f'xsp{i}', inst, 16)

    if DBG == 'QT':
        r.wait('g', 'P2sp10')
        r.wait('g', 'P2sp11')
        inst = g.dma_start(out=out_d[0:1024, :],
                           in_=qt_d.rearrange("a (b c) -> (a b) c", c=1024)) if g else None
        r.mark('lst', 'dbgqt', inst, 16)
        inst = g.dma_start(out=out_d[1024:2048, :],
                           in_=kt_d.rearrange("a (b c) -> (a b) c", c=1024)) if g else None
        r.mark('lst', 'dbgkt', inst, 16)
        r.wait('g', 'dbgkt')
        return



    for h in range(H):
        QTh = QTs[h % 2][0:64, :]
        KTh = KTs[h % 2][0:64, :]
        if h > 0:
            r.wait('y', f'h{h - 1}vtp')
        inst = y.dma_start(out=VTs[0:64, :], in_=vt_d[64 * h:64 * h + 64, :]) if y else None
        r.mark('lvt', f'vts{h}', inst, 16)

        for i in range(NT):
            idx = i
            which = idx % 2
            psrc = pzBig if which == 0 else pzB2
            if idx == 0:
                r.wait('t', f'sl{h}k')
                if h == 0:
                    r.wait('t', 'P2ev10')
                    r.wait('t', 'P2ev11')
                else:
                    r.wait('t', f'h{h - 1}ubce3')
            if idx == 1 and h > 0:
                r.wait('t', f'h{h - 1}aslc')
            if idx >= 2:
                r.wait('t', f'h{h}ex{idx - 2}')
            for c in range(4):
                inst = t.matmul(psrc[:, c * 512:(c + 1) * 512],
                                QTh[:, i * 128:(i + 1) * 128],
                                KTh[:, c * 512:(c + 1) * 512],
                                start=True, stop=True) if t else None
            r.mark('tm', f'h{h}lg{idx}', inst)
            r.wait('s', f'h{h}lg{idx}')
            if h == 0 and idx == 8:
                r.wait('s', 'xsp15')
            if idx == 0 and h > 0:
                r.wait('s', f'h{h - 1}p{NS - 1}')
            with LP():
                inst = sc.activation(out=Atile[i], in_=psrc[:, :], func=AF.Exp,
                                     scale=0.125) if sc else None
            r.mark('sm', f'h{h}ex{idx}', inst)
        if h + 1 < H:
            if h >= 1:
                r.wait('y', f'h{h - 1}lg{NT - 1}')
            slice_load(h + 1)
        r.wait('t', f'h{h}ex{NT - 1}')
        r.wait('t', f'h{h}ex{NT - 2}')
        for ti in range(NT):
            for half in range(2):
                if ti > 0:
                    r.wait('t', f'h{h}bev{ti - 1}_{half}')
                elif h > 0:
                    r.wait('t', f'h{h - 1}gve{half}' if False else f'h{h - 1}aslc')
                for j in range(8 * half, 8 * half + 8):
                    inst = t.transpose(pb[half][:, (j - 8 * half) * 128:(j - 8 * half + 1) * 128],
                                       Atile[j][:, ti * 128:(ti + 1) * 128], ident[:, :]) if t else None
                r.mark('tm', f'h{h}btp{ti}_{half}', inst)
                if half == 0:
                    r.wait('v', f'h{h}btp{ti}_{half}')
                    with LP():
                        inst = v.tensor_copy(out=Btile[ti][:, 0:1024],
                                             in_=pb[0][:, :]) if v else None
                    r.mark('vm', f'h{h}bev{ti}_{half}', inst)
                else:
                    r.wait('s', f'h{h}btp{ti}_{half}')
                    with LP():
                        inst = sc.copy(out=Btile[ti][:, 1024:2048],
                                       in_=pb[1][:, :]) if sc else None
                    r.mark('sm', f'h{h}bev{ti}_{half}', inst)

        inst = v.memset(gcol16[:, :], 1.0) if v else None
        r.mark('vm', f'h{h}rinit', inst)
        for s in range(NS):
            mat = Btile if s % 2 == 0 else Atile
            vecap = gcol16 if s % 2 == 0 else ucol16
            r.wait('t', f'h{h}r{s - 1}' if s > 0 else f'h{h}rinit')
            if s > 0:
                r.wait('t', f'h{h}evs{s - 1}_1')
                r.wait('t', f'h{h}evv{s - 1}_1')
            for ti in range(NT):
                if s == 0:
                    r.wait('t', f'h{h}bev{ti}_0')
                    r.wait('t', f'h{h}bev{ti}_1')
                for c in range(4):
                    inst = t.matmul(pz[c][32 * c:32 * c + 1, :], vecap[:, ti:ti + 1],
                                    mat[ti][:, c * 512:(c + 1) * 512],
                                    start=(ti == 0), stop=(ti == NT - 1),
                                    tile_position=(0, 32 * c)) if t else None
            r.mark('tm', f'h{h}p{s}', inst)
            for w in range(NWARM):
                if t:
                    t.matmul(pwarm[0:1, :], gcol16[:, 0:1],
                             Btile[0][:, (w % 4) * 512:(w % 4 + 1) * 512],
                             start=True, stop=True, tile_position=(0, 0))
            if True:
                for gq in range(4):
                    if gq < 2:
                        r.wait('t', f'h{h}evs{s}_{gq}')
                    else:
                        r.wait('t', f'h{h}evv{s}_{gq - 2}')
                    if gq == 0 and s > 0:
                        r.wait('t', f'h{h}r{s - 1}')
                    p = 32 * gq
                    for j in range(4 * gq, 4 * gq + 4):
                        inst = t.matmul(pt[:, j:j + 1],
                                        yrow[p:p + 1, j * 128:(j + 1) * 128],
                                        ones16[p:p + 1, 0:1],
                                        start=True, stop=True, tile_position=(p, 0)) if t else None
                    r.mark('tm', f'h{h}tp{s}_{gq}', inst)
            r.wait('s', f'h{h}p{s}')
            for c in range(2):
                with LP():
                    inst = sc.copy(out=yrow[32 * c:32 * c + 1, c * 512:(c + 1) * 512],
                                   in_=pz[c][32 * c:32 * c + 1, :]) if sc else None
                r.mark('sm', f'h{h}evs{s}_{c}', inst)
            r.wait('v', f'h{h}p{s}')
            for c in range(2, 4):
                with LP():
                    inst = v.tensor_copy(out=yrow[32 * c:32 * c + 1, c * 512:(c + 1) * 512],
                                         in_=pz[c][32 * c:32 * c + 1, :]) if v else None
                r.mark('vm', f'h{h}evv{s}_{c - 2}', inst)
            for gq in range(4):
                r.wait('v', f'h{h}tp{s}_{gq}')
            if s == NS - 1:
                inst = v.reciprocal(out=gcol32[:, :], in_=pt[:, :]) if v else None
            else:
                col16 = ucol16 if s % 2 == 0 else gcol16
                with LP():
                    inst = v.reciprocal(out=col16[:, :], in_=pt[:, :]) if v else None
            r.mark('vm', f'h{h}r{s}', inst)
            if s == 2:
                r.wait('t', f'vts{h}')
                if h > 0:
                    r.wait('t', f'h{h - 1}vke')
                for j in range(16):
                    inst = t.transpose(pb[0][:, j * 64:(j + 1) * 64],
                                       VTs[0:64, j * 128:(j + 1) * 128], ident[0:64, 0:64]) if t else None
                r.mark('tm', f'h{h}vtp', inst)
                r.wait('v', f'h{h}vtp')
                with LP():
                    inst = v.tensor_copy(out=Vk[:, :], in_=pb[0][:, 0:1024]) if v else None
                r.mark('vm', f'h{h}vke', inst)
            if s == NS - 1:
                r.wait('t', f'h{h}vke')
                for j in range(16):
                    dst = pb[0] if j < 8 else pb[1]
                    inst = t.transpose(dst[0:1, (j % 8) * 128:(j % 8 + 1) * 128],
                                       ucol16[:, j:j + 1], ident[:, :]) if t else None
                r.mark('tm', f'h{h}utp', inst)
                r.wait('s', f'h{h}utp')
                for half in range(2):
                    with LP():
                        inst = sc.copy(out=urow16[0:1, half * 1024:(half + 1) * 1024],
                                       in_=(pb[0] if half == 0 else pb[1])[0:1, :]) if sc else None
                r.mark('sm', f'h{h}uev', inst)
                r.wait('t', f'h{h}uev')
                r.wait('t', f'h{h}r{NS - 1}')
                r.wait('t', f'h{h}evs{NS - 1}_1')
                r.wait('t', f'h{h}evv{NS - 1}_1')
                for c in range(4):
                    inst = t.matmul(pz[c][0:64, :], ones16[0:1, 0:64],
                                    urow16[0:1, c * 512:(c + 1) * 512],
                                    start=True, stop=True, tile_position=(0, 0)) if t else None
                    r.mark('tm', f'h{h}ubc{c}', inst)
                    r.wait('v', f'h{h}ubc{c}')
                    with LP():
                        inst = v.tensor_copy(out=ubc16[0:64, c * 512:(c + 1) * 512],
                                             in_=pz[c][0:64, :]) if v else None
                    r.mark('vm', f'h{h}ubce{c}', inst)

        r.wait('v', f'h{h}r{NS - 1}')
        r.wait('v', f'h{h}vke')
        for j in range(16):
            with LP():
                inst = v.tensor_scalar_mul(gV[:, j * 64:(j + 1) * 64],
                                           Vk[:, j * 64:(j + 1) * 64],
                                           gcol32[:, j:j + 1]) if v else None
        r.mark('vm', f'h{h}gvs', inst)
        r.wait('t', f'h{h}gvs')
        r.wait('t', f'h{h}r{NS - 1}_3' if False else f'h{h}uev')
        for c in range(4):
            for ti in range(NT):
                inst = t.matmul(pzc[c][0:64, :], gV[:, ti * 64:(ti + 1) * 64],
                                Btile[ti][:, c * 512:(c + 1) * 512],
                                start=(ti == 0), stop=(ti == NT - 1)) if t else None
        r.mark('tm', f'h{h}avmm', inst)
        r.wait('v', f'h{h}avmm')
        for c in range(4):
            r.wait('v', f'h{h}ubce{c}')
        if h > 0:
            r.wait('v', f'h{h - 1}asp')
        with LP():
            inst = v.tensor_mul(aslc[0:64, :], pzB2[0:64, :], ubc16[0:64, :]) if v else None
        r.mark('vm', f'h{h}aslc', inst)
        r.wait('y', f'h{h}aslc')
        inst = y.dma_start(out=at_d[64 * h:64 * h + 64, :], in_=aslc[0:64, :]) if y else None
        r.mark('lat', f'h{h}asp', inst, 16)

    if DBG == 'H0':
        r.wait('g', 'h0asp')
        inst = g.dma_start(out=out_d[0:128, :],
                           in_=at_d[0:64, :].rearrange("a (b c) -> (a b) c", c=1024)) if g else None
        r.mark('lst', 'dbgh0', inst, 16)
        r.wait('g', 'dbgh0')
        return

    r.wait('y', f'h{H - 1}asp')
    for j in range(4):
        inst = y.dma_start(out=atT[j], in_=at_d[j * 128:(j + 1) * 128, :]) if y else None
        r.mark('lat', f'atl{j}', inst, 16)
    for i in range(NT):
        inst = y.dma_start(out=xt(i), in_=xs_d[i * 128:(i + 1) * 128, :]) if y else None
        r.mark('ldx', f'xrl{i}', inst, 16)
    r.wait('g', f'h{H - 1}aslc')
    for j in range(4):
        inst = g.dma_start(out=Wot[j], in_=wo_d[j * 128:(j + 1) * 128, :]) if g else None
        r.mark('ldw', f'wo{j}', inst, 16)
    inst = g.dma_start(out=Woutb2[0:64, :], in_=wout_d[:, :]) if g else None
    r.mark('ldw', 'wout2', inst, 16)
    inst = g.dma_start(out=Woutb2[64:128, :], in_=wout_d[:, :]) if g else None
    r.mark('ldw', 'wout2d', inst, 16)
    for k in range(8):
        inst = g.dma_start(out=Winb2[k], in_=win_d[k * 128:(k + 1) * 128, :]) if g else None
        r.mark('ldw', f'win2_{k}', inst, 16)

    for i in range(NT):
        which = i % 2
        if i >= 2:
            r.wait('t', f'P4xa{i - 2}')
        if i == 0:
            r.wait('t', 'atl3')
            r.wait('t', 'win2_7')
            r.wait('t', 'bo16')
            r.wait('t', f'h{H - 1}aslc')
        for nch in range(2):
            for j in range(4):
                inst = t.matmul(pzh[which][:, nch * 512:(nch + 1) * 512],
                                atT[j][:, i * 128:(i + 1) * 128],
                                Wot[j][:, nch * 512:(nch + 1) * 512],
                                start=(j == 0), stop=False) if t else None
            inst = t.matmul(pzh[which][:, nch * 512:(nch + 1) * 512],
                            ones16[0:1, 0:128], bo16[0:1, nch * 512:(nch + 1) * 512],
                            start=False, stop=True, tile_position=(0, 0)) if t else None
        r.mark('tm', f'P4mm{i}', inst)
        r.wait('v', f'P4mm{i}')
        r.wait('v', 'xrl15')
        inst = v.tensor_add(xt(i), xt(i), pzh[which][:, :]) if v else None
        r.mark('vm', f'P4xa{i}', inst)

    if DBG == 'P4':
        dbg_dump_x('P4xa')
        return

    layernorm('L5', lambda i: f'P4xa{i}', guard='P4mm15')
    transpose_lnx('L5')
    ffn('L5', Winb2, Woutb2, 'win2_7', 'win2_7', pzfree='P4xa15')

    for i in range(NT):
        r.wait('y', f'L5xadd{i}')
        inst = y.dma_start(out=out_d[i * 128:(i + 1) * 128, :], in_=xt(i)) if y else None
        r.mark('lst', f'st{i}', inst, 16)
    r.wait('y', f'st{NT - 1}')


def build_nc():
    nc = bass.Bass("TRN2", target_bir_lowering=False, debug=False)
    names = [("x_in", [T, D], F32, "ExternalInput"),
             ("wq", [D, 512], F32, "ExternalInput"),
             ("bq", [4, 128], F32, "ExternalInput"),
             ("wk", [D, 512], F32, "ExternalInput"),
             ("bk", [4, 128], F32, "ExternalInput"),
             ("wv", [D, 512], F32, "ExternalInput"),
             ("bv", [4, 128], F32, "ExternalInput"),
             ("wo", [512, D], F32, "ExternalInput"),
             ("bo", [D], F32, "ExternalInput"),
             ("win", [D, 64], F32, "ExternalInput"),
             ("binp", [128, 1], F32, "ExternalInput"),
             ("w1", [128, 16], F32, "ExternalInput"),
             ("b1", [128, 16], F32, "ExternalInput"),
             ("w2", [128, 16], F32, "ExternalInput"),
             ("b2", [64, 1], F32, "ExternalInput"),
             ("wout", [64, D], F32, "ExternalInput"),
             ("bout", [D], F32, "ExternalInput"),
             ("identb", [128, 128], BF16, "ExternalInput"),
             ("out", [T, D], F32, "ExternalOutput"),
             ("xs", [T, D], F32, "Internal"),
             ("qts", [512, T], BF16, "Internal"),
             ("kts", [512, T], BF16, "Internal"),
             ("vts", [512, T], BF16, "Internal"),
             ("ats", [512, T], BF16, "Internal")]
    io = [nc.dram_tensor(n, s, d, kind=k).ap() for n, s, d, k in names]

    with contextlib.ExitStack() as stack:
        e = stack.enter_context
        A = e(nc.sbuf_tensor("Areg", [128, 32768], BF16))
        Bm = e(nc.sbuf_tensor("Breg", [128, 32768], BF16))
        Q = e(nc.sbuf_tensor("Qreg", [128, 30208], BF16))
        M = e(nc.sbuf_tensor("Mreg", [128, 9216], BF16))
        pzBig = e(nc.psum_tensor("pzBig", [128, 2048], F32))
        pzB2 = e(nc.psum_tensor("pzB2", [128, 2048], F32))
        sems = {nm: e(nc.semaphore(nm)) for nm in
                ('ldx', 'ldx2', 'lxs', 'ldsm', 'ldw', 'lspA', 'lspB', 'lslA', 'lslB',
                 'lvt', 'lat', 'lst', 'tm', 'sm', 'vm')}
        block = e(nc.Block())
        bufs = (A[:, :], Bm[:, :], Q[:, :], M[:, :], pzBig[:, :], pzB2[:, :])

        planner = Run()
        body(planner, nc, io, bufs)
        plan = planner.plan

        @block.tensor
        def _(E):
            body(Run('t', plan, sems, E), nc, io, bufs)

        @block.scalar
        def _(E):
            body(Run('s', plan, sems, E), nc, io, bufs)

        @block.vector
        def _(E):
            body(Run('v', plan, sems, E), nc, io, bufs)

        @block.sync
        def _(E):
            body(Run('y', plan, sems, E), nc, io, bufs)

        @block.gpsimd
        def _(E):
            body(Run('g', plan, sems, E), nc, io, bufs)

    return nc


def kernel(L0_attn_x, L0_attn_Wq, L0_attn_bq, L0_attn_Wk, L0_attn_bk,
           L0_attn_Wv, L0_attn_bv, L0_attn_Wo, L0_attn_bo,
           L0_ffn_Win, L0_ffn_bin_, L0_ffn_W1, L0_ffn_b1, L0_ffn_W2,
           L0_ffn_b2, L0_ffn_Wout, L0_ffn_bout):
    x = np.ascontiguousarray(np.asarray(L0_attn_x, dtype=np.float32))
    common = {
        "wq": np.ascontiguousarray(np.asarray(L0_attn_Wq, np.float32)),
        "bq": np.ascontiguousarray(np.asarray(L0_attn_bq, np.float32).reshape(4, 128)),
        "wk": np.ascontiguousarray(np.asarray(L0_attn_Wk, np.float32)),
        "bk": np.ascontiguousarray(np.asarray(L0_attn_bk, np.float32).reshape(4, 128)),
        "wv": np.ascontiguousarray(np.asarray(L0_attn_Wv, np.float32)),
        "bv": np.ascontiguousarray(np.asarray(L0_attn_bv, np.float32).reshape(4, 128)),
        "wo": np.ascontiguousarray(np.asarray(L0_attn_Wo, np.float32)),
        "bo": np.ascontiguousarray(np.asarray(L0_attn_bo, np.float32)),
        "win": np.ascontiguousarray(np.asarray(L0_ffn_Win, np.float32)),
        "binp": np.ascontiguousarray(np.tile(np.asarray(L0_ffn_bin_, np.float32).reshape(64, 1), (2, 1))),
        "w1": np.ascontiguousarray(np.tile(np.asarray(L0_ffn_W1, np.float32), (2, 1))),
        "b1": np.ascontiguousarray(np.tile(np.asarray(L0_ffn_b1, np.float32), (2, 1))),
        "w2": np.ascontiguousarray(np.tile(np.asarray(L0_ffn_W2, np.float32), (2, 1))),
        "b2": np.ascontiguousarray(np.asarray(L0_ffn_b2, np.float32).reshape(64, 1)),
        "wout": np.ascontiguousarray(np.asarray(L0_ffn_Wout, np.float32)),
        "bout": np.ascontiguousarray(np.asarray(L0_ffn_bout, np.float32)),
        "identb": np.eye(128, dtype=ml_dtypes.bfloat16),
    }
    nc = build_nc()
    in_maps = [dict(common, x_in=np.ascontiguousarray(x[b])) for b in range(8)]
    res = run_bass_kernel_spmd(nc, in_maps, core_ids=list(range(8)))
    out = np.stack([np.asarray(res.results[b]["out"]) for b in range(8)], axis=0)
    return out.astype(np.float32)


# version 19
# speedup vs baseline: 1.0104x; 1.0104x over previous
import contextlib
import numpy as np
import ml_dtypes
import concourse.bass as bass
import concourse.mybir as mybir
from concourse.bass_utils import run_bass_kernel_spmd

F32 = mybir.dt.float32
BF16 = mybir.dt.bfloat16
AF = mybir.ActivationFunctionType
OP = mybir.AluOpType
AX = mybir.AxisListType.X

DBG = None
T = 2048
D = 1024
H = 8
NT = 16
ITERS = 20
NS = 2 * ITERS
NWARM = 6


class Run:
    def __init__(self, eng=None, plan=None, sems=None, E=None):
        self.eng = eng
        self.planning = plan is None
        self.plan = {} if plan is None else plan
        self.counts = {}
        self.sems = sems
        self.E = E

    def mark(self, sem, event, inst, n=1):
        if self.planning:
            c = self.counts.get(sem, 0) + n
            self.counts[sem] = c
            assert event not in self.plan, f"dup event {event}"
            self.plan[event] = (sem, c)
        if inst is not None:
            inst.then_inc(self.sems[sem], n)

    def wait(self, eng, event):
        if self.planning or eng != self.eng:
            return
        sem, c = self.plan[event]
        self.E.wait_ge(self.sems[sem], c)

    def own(self, eng):
        return (not self.planning) and eng == self.eng


def body(r, nc, io, bufs):
    t = r.E if r.own('t') else None
    sc = r.E if r.own('s') else None
    v = r.E if r.own('v') else None
    y = r.E if r.own('y') else None
    g = r.E if r.own('g') else None

    A, Bm, Q, M, pzBig, pzB2 = bufs

    Af32 = A.bitcast(F32)
    Bf32 = Bm.bitcast(F32)
    LNx = [A[:, i * 1024:(i + 1) * 1024] for i in range(NT)]
    LNxT = [Bm[:, j * T:(j + 1) * T] for j in range(8)]
    Atile = [A[:, i * T:(i + 1) * T] for i in range(NT)]
    Btile = [Bm[:, i * T:(i + 1) * T] for i in range(NT)]
    atT = [A[:, j * T:(j + 1) * T] for j in range(4)]

    def xt(i):
        if i < 8:
            return Af32[:, 8192 + i * 1024: 8192 + (i + 1) * 1024]
        return Bf32[:, 8192 + (i - 8) * 1024: 8192 + (i - 7) * 1024]

    s2 = Af32[:, 0:1024]
    hb = [Af32[:, 1024:2048], Af32[:, 2048:3072]]
    gA = Af32[:, 3072:4096]
    gB = Af32[:, 4096:5120]
    bout_r = Af32[0:1, 5120:6144]
    hbe_r = Af32[0:1, 6144:7168]
    tmp_r = Af32[0:1, 7168:8192]
    bo_r = Bf32[0:1, 512:1536]
    Woutb0 = Q[:, 8192:9216]

    stg = [Q[:, 0:2048], Q[:, 2048:4096]]
    QTs = [Q[:, 0:2048], Q[:, 2048:4096]]
    KTs = [Q[:, 4096:6144], Q[:, 6144:8192]]
    VTs = Q[:, 8192:10240]
    Vk = Q[:, 10240:11264]
    gV = Q[:, 14336:15360]
    aslc = Q[:, 15360:17408]
    ubc16 = Q[:, 12288:14336]
    urow16 = Q[:, 21504:23552]
    Wqt = [Q[:, 17408 + k * 512:17408 + (k + 1) * 512] for k in range(8)]
    Wkt = [Q[:, 21504 + k * 512:21504 + (k + 1) * 512] for k in range(8)]
    Wvt = [Q[:, 25600 + k * 512:25600 + (k + 1) * 512] for k in range(8)]
    Wint = [Q[:, 29696 + k * 64:29696 + (k + 1) * 64] for k in range(8)]
    ubc32 = Q[:, 17408:21504].bitcast(F32)
    Wot = [Q[:, j * 1024:(j + 1) * 1024] for j in range(4)]
    Woutb2 = Q[:, 4096:5120]
    Winb2 = [Q[:, 5120 + k * 64:5120 + (k + 1) * 64] for k in range(8)]

    yrow = M[:, 0:2048]
    ident = M[:, 2048:2176]
    ones16 = M[:, 2176:2304]
    ucol16 = M[:, 2304:2320]
    gcol16 = M[:, 2320:2336]
    bo16 = M[:, 2336:3360]
    hbe16 = M[:, 3360:4384]
    b2c16 = M[:, 4384:4385]
    Mf = M[:, 4608:9216].bitcast(F32)
    ones32 = Mf[:, 0:128]
    gcol32 = Mf[:, 128:144]
    bnst = Mf[:, 144:156]
    bnagg = Mf[:, 160:192]
    mean = Mf[:, 1152:1168]
    var = Mf[:, 1168:1184]
    rstd = Mf[:, 1184:1200]
    nms = Mf[:, 1200:1216]
    msq = Mf[:, 1216:1232]
    bqc = Mf[:, 1232:1236]
    bkc = Mf[:, 1236:1240]
    bvc = Mf[:, 1240:1244]
    binc = Mf[:, 1244:1245]
    b2c = Mf[:, 1245:1246]
    w1c = Mf[:, 1248:1264]
    b1c = Mf[:, 1264:1280]
    w2h = Mf[:, 1280:1296]
    w2r = Mf[:, 1296:1312]

    pz = [pzBig[:, c * 512:(c + 1) * 512] for c in range(4)]
    pzh = [pzBig[:, 0:1024], pzBig[:, 1024:2048]]
    pt = pzB2[:, 0:16]
    pwarm = pzB2[:, 512:1024]
    pb = [pzB2[:, 1024:1536].bitcast(BF16), pzB2[:, 1536:2048].bitcast(BF16)]
    pzc = [pzB2[:, c * 512:(c + 1) * 512] for c in range(4)]

    (x_d, wq_d, bq_d, wk_d, bk_d, wv_d, bv_d, wo_d, bo_d,
     win_d, bin_d, w1_d, b1_d, w2_d, b2_d, wout_d, bout_d,
     ident_d, out_d, xs_d, qt_d, kt_d, vt_d, at_d) = io

    LP = lambda: nc.allow_low_precision(reason="bf16 matmul-operand staging (blessed)")

    for i in range(NT):
        inst = y.dma_start(out=xt(i), in_=x_d[i * 128:(i + 1) * 128, :]) if y else None
        r.mark('ldx' if i < 8 else 'ldx2', f'x{i}', inst, 16)
    inst = y.dma_start(out=ident[:, :], in_=ident_d[:, :]) if y else None
    r.mark('ldsm', 'ident', inst, 16)
    for nm, dst, src in (('bq', bqc, bq_d), ('bk', bkc, bk_d), ('bv', bvc, bv_d)):
        inst = y.dma_start(out=dst[:, :], in_=src.rearrange("m p -> p m"),
                           allow_slow_non_contiguous=True) if y else None
        r.mark('ldsm', nm, inst, 16)
    inst = y.dma_start(out=binc[:, :], in_=bin_d[:, :]) if y else None
    r.mark('ldsm', 'bin0', inst, 16)
    inst = y.dma_start(out=b2c[0:64, :], in_=b2_d[0:64, :]) if y else None
    r.mark('ldsm', 'b2', inst, 16)
    for nm, dst, src in (('w1', w1c, w1_d), ('b1', b1c, b1_d), ('w2', w2r, w2_d)):
        inst = y.dma_start(out=dst[:, :], in_=src[:, :]) if y else None
        r.mark('ldsm', f'{nm}0', inst, 16)
    inst = y.dma_start(out=bo_r, in_=bo_d.rearrange("(a d) -> a d", a=1)) if y else None
    r.mark('ldsm', 'bo', inst, 16)
    inst = y.dma_start(out=bout_r, in_=bout_d.rearrange("(a d) -> a d", a=1)) if y else None
    r.mark('ldsm', 'bout', inst, 16)
    for k in range(8):
        inst = g.dma_start(out=Wqt[k], in_=wq_d[k * 128:(k + 1) * 128, :]) if g else None
        r.mark('ldw', f'wq{k}', inst, 16)
        inst = g.dma_start(out=Wkt[k], in_=wk_d[k * 128:(k + 1) * 128, :]) if g else None
        r.mark('ldw', f'wk{k}', inst, 16)
        inst = g.dma_start(out=Wvt[k], in_=wv_d[k * 128:(k + 1) * 128, :]) if g else None
        r.mark('ldw', f'wv{k}', inst, 16)
        inst = g.dma_start(out=Wint[k], in_=win_d[k * 128:(k + 1) * 128, :]) if g else None
        r.mark('ldw', f'win{k}', inst, 16)
    inst = g.dma_start(out=Woutb0[0:64, :], in_=wout_d[:, :]) if g else None
    r.mark('ldw', 'wout', inst, 16)
    inst = g.dma_start(out=Woutb0[64:128, :], in_=wout_d[:, :]) if g else None
    r.mark('ldw', 'woutd', inst, 16)

    if v:
        v.memset(ones16[:, :], 1.0)
        v.memset(ones32[:, :], 1.0)
    inst = v.memset(yrow[:, :], 0.0) if v else None
    r.mark('vm', 'consts', inst)
    r.wait('v', 'bout')
    inst = v.tensor_scalar_mul(w2h[:, :], w2r[:, :], 0.5) if v else None
    r.mark('vm', 'w2half', inst)

    r.wait('v', 'bout')
    with LP():
        inst = v.tensor_copy(out=b2c16[0:64, :], in_=b2c[0:64, :]) if v else None
    r.mark('vm', 'b2c16', inst)
    r.wait('t', 'b2c16')
    r.wait('t', 'woutd')
    for c in range(2):
        inst = t.matmul(pzc[c][0:1, :], b2c16[0:64, 0:1], Woutb0[0:64, c * 512:(c + 1) * 512],
                        start=True, stop=True) if t else None
        r.mark('tm', f'wtb2_{c}', inst)
    r.wait('s', 'bout')
    inst = sc.mul(tmp_r, bout_r, 0.5) if sc else None
    r.mark('sm', 'halfbout', inst)
    r.wait('v', 'wtb2_0')
    r.wait('v', 'wtb2_1')
    r.wait('v', 'halfbout')
    for c in range(2):
        with LP():
            inst = v.scalar_tensor_tensor(out=hbe_r[:, c * 512:(c + 1) * 512],
                                          in0=pzc[c][0:1, :], scalar=0.5,
                                          in1=tmp_r[:, c * 512:(c + 1) * 512],
                                          op0=OP.mult, op1=OP.add) if v else None
        r.mark('vm', f'hbe{c}', inst)
    r.wait('s', 'hbe0')
    r.wait('s', 'hbe1')
    with LP():
        inst = sc.copy(out=hbe16[0:1, :], in_=hbe_r) if sc else None
    r.mark('sm', 'hbe16', inst)
    r.wait('s', 'bout')
    with LP():
        inst = sc.copy(out=bo16[0:1, :], in_=bo_r) if sc else None
    r.mark('sm', 'bo16', inst)

    def warm(n):
        if t:
            for w in range(n):
                t.matmul(pwarm[0:1, :], ones16[:, 0:1], yrow[:, (w % 3) * 512:(w % 3) * 512 + 512],
                         start=True, stop=True, tile_position=(0, 0))

    def layernorm(tag, xev, guard=None):
        meanv = bnagg[:, 0:32:2]
        varv = bnagg[:, 1:32:2]
        for i in range(NT):
            if i == 0 and guard is not None:
                r.wait('v', guard)
                r.wait('s', guard)
            r.wait('v', xev(i))
            if i > 0:
                r.wait('v', f'{tag}ag{i - 1}')
            inst = v.bn_stats(bnst[:, 0:6], xt(i)[:, 0:512]) if v else None
            r.mark('vm', f'{tag}ba{i}', inst)
            r.wait('v', f'{tag}ba{i}')
            inst = v.bn_stats(bnst[:, 6:12], xt(i)[:, 512:1024]) if v else None
            r.mark('vm', f'{tag}bb{i}', inst)
            r.wait('v', f'{tag}bb{i}')
            inst = v.bn_aggr(bnagg[:, 2 * i:2 * i + 2], bnst[:, 0:12]) if v else None
            r.mark('vm', f'{tag}ag{i}', inst)
        r.wait('v', f'{tag}ag{NT - 1}')
        inst = v.tensor_scalar_add(var[:, :], varv, 1e-5) if v else None
        r.mark('vm', f'{tag}m5', inst)
        r.wait('s', f'{tag}m5')
        inst = sc.sqrt(msq[:, :], var[:, :]) if sc else None
        r.mark('sm', f'{tag}m6', inst)
        r.wait('v', f'{tag}m6')
        inst = v.reciprocal(rstd[:, :], msq[:, :]) if v else None
        r.mark('vm', f'{tag}m7', inst)
        r.wait('v', f'{tag}m7')
        with LP():
            inst = v.scalar_tensor_tensor(out=nms[:, :], in0=meanv, scalar=-1.0,
                                          in1=rstd[:, :], op0=OP.mult, op1=OP.mult) if v else None
        r.mark('vm', f'{tag}m8', inst)
        for i in range(NT):
            r.wait('s', f'{tag}m8')
            with LP():
                inst = sc.activation(out=LNx[i], in_=xt(i), func=AF.Identity,
                                     scale=rstd[:, i:i + 1], bias=nms[:, i:i + 1]) if sc else None
            r.mark('sm', f'{tag}ln{i}', inst)

    def transpose_lnx(tag):
        r.wait('t', 'bout')
        for j in range(8):
            for half in range(2):
                idx = j * 2 + half
                which = idx % 2
                if idx >= 2:
                    r.wait('t', f'{tag}tpe{idx - 2}')
                for i in range(8 * half, 8 * half + 8):
                    r.wait('t', f'{tag}ln{i}')
                    inst = t.transpose(pb[which][:, (i - 8 * half) * 128:(i - 8 * half + 1) * 128],
                                       LNx[i][:, j * 128:(j + 1) * 128], ident[:, :]) if t else None
                r.mark('tm', f'{tag}tp{idx}', inst)
                r.wait('v', f'{tag}tp{idx}')
                inst = v.tensor_copy(out=LNxT[j][:, half * 1024:(half + 1) * 1024],
                                     in_=pb[which][:, :]) if v else None
                r.mark('vm', f'{tag}tpe{idx}', inst)

    def ffn(tag, win_tiles, woutb, win_ev, wout_ev, pzfree=None):
        for c in range(4):
            r.wait('t', win_ev)
            if c == 0 and pzfree is not None:
                r.wait('t', pzfree)
            for k in range(8):
                r.wait('t', f'{tag}tpe{k * 2 + (c // 2)}')
                inst = t.matmul(pz[c][0:64, :], win_tiles[k][:, :],
                                LNxT[k][:, c * 512:(c + 1) * 512],
                                start=(k == 0), stop=(k == 7)) if t else None
            r.mark('tm', f'{tag}sin{c}', inst)
        for c in range(4):
            r.wait('s', f'{tag}sin{c}')
            r.wait('s', 'bout')
            half, col = c % 2, (c // 2) * 512
            inst = sc.activation(out=s2[64 * half:64 * half + 64, col:col + 512],
                                 in_=pz[c][0:64, :], func=AF.Identity,
                                 bias=binc[64 * half:64 * half + 64, :]) if sc else None
            r.mark('sm', f'{tag}s2_{c}', inst)
        for i in range(16):
            for c in range(4):
                r.wait('s', f'{tag}s2_{c}')
            r.wait('s', 'bout')
            if i >= 2:
                r.wait('s', f'{tag}g{i - 2}')
            inst = sc.activation(out=hb[i % 2], in_=s2[:, :], func=AF.Relu,
                                 scale=w1c[:, i:i + 1], bias=b1c[:, i:i + 1]) if sc else None
            r.mark('sm', f'{tag}h{i}', inst)
            r.wait('v', f'{tag}h{i}')
            gsrc, gdst = (gA, gB) if i % 2 else (gB, gA)
            if i == 0:
                r.wait('v', 'w2half')
                inst = v.tensor_scalar_mul(gdst, hb[i % 2], w2h[:, 0:1]) if v else None
            else:
                r.wait('v', f'{tag}g{i - 1}')
                with LP():
                    inst = v.scalar_tensor_tensor(out=gdst, in0=hb[i % 2], scalar=w2h[:, i:i + 1],
                                                  in1=gsrc, op0=OP.mult, op1=OP.add) if v else None
            r.mark('vm', f'{tag}g{i}', inst)
        gfin = gB
        if DBG == 'S2' and tag == 'L1':
            for c in range(4):
                r.wait('y', f'{tag}s2_{c}')
            inst = y.dma_start(out=out_d[0:128, :], in_=s2) if y else None
            r.mark('lst', 'dbgs2', inst, 16)
            r.wait('v', f'{tag}g15')
            r.wait('y', f'{tag}g15')
            inst = y.dma_start(out=out_d[128:256, :], in_=gfin) if y else None
            r.mark('lst', 'dbgg', inst, 16)
            r.wait('y', 'dbgg')
            return True
        r.wait('v', f'{tag}g15')
        with LP():
            inst = v.tensor_copy(out=stg[0][:, 0:1024], in_=gfin) if v else None
        r.mark('vm', f'{tag}g16', inst)
        gb16 = stg[0][:, 0:1024]
        for i in range(NT):
            half = (i // 4) % 2
            col = (i // 8) * 512 + (i % 4) * 128
            which = i % 2
            if i >= 2:
                r.wait('t', f'{tag}xadd{i - 2}')
            r.wait('t', f'{tag}g16')
            r.wait('t', 'hbe16')
            r.wait('t', 'consts')
            r.wait('t', wout_ev)
            for nch in range(2):
                inst = t.matmul(pzh[which][:, nch * 512:(nch + 1) * 512],
                                gb16[64 * half:64 * half + 64, col:col + 128],
                                woutb[64 * half:64 * half + 64, nch * 512:(nch + 1) * 512],
                                start=True, stop=False) if t else None
                inst = t.matmul(pzh[which][:, nch * 512:(nch + 1) * 512],
                                ones16[0:1, 0:128], hbe16[0:1, nch * 512:(nch + 1) * 512],
                                start=False, stop=True, tile_position=(0, 0)) if t else None
            r.mark('tm', f'{tag}out{i}', inst)
            r.wait('v', f'{tag}out{i}')
            inst = v.tensor_add(xt(i), xt(i), pzh[which][:, :]) if v else None
            r.mark('vm', f'{tag}xadd{i}', inst)

    if DBG == 'WIN':
        for k in range(8):
            r.wait('g', 'woutd')
            inst = g.dma_start(out=out_d[k * 128:(k + 1) * 128, 0:64], in_=Wint[k]) if g else None
            r.mark('lst', f'dbgw{k}', inst, 16)
        r.wait('g', 'dbgw7')
        return

    layernorm('L1', lambda i: 'x7' if i < 8 else 'x15')
    if DBG == 'LN1':
        for i in range(NT):
            r.wait('g', f'L1ln{i}')
            inst = g.dma_start(out=out_d[i * 128:(i + 1) * 128, :], in_=LNx[i]) if g else None
            r.mark('lst', f'dbgln{i}', inst, 16)
        r.wait('g', f'dbgln{NT - 1}')
        return
    transpose_lnx('L1')
    if DBG == 'LNT':
        for j in range(8):
            r.wait('g', f'L1tpe{j * 2}')
            r.wait('g', f'L1tpe{j * 2 + 1}')
            inst = g.dma_start(out=out_d[j * 256:(j + 1) * 256, :].rearrange("(p b) c -> p (b c)", b=2),
                               in_=LNxT[j]) if g else None
            r.mark('lst', f'dbglnt{j}', inst, 16)
        r.wait('g', 'dbglnt7')
        return
    if ffn('L1', Wint, Woutb0, 'woutd', 'woutd'):
        return

    def dbg_dump_x(pref):
        for i in range(NT):
            r.wait('y', f'{pref}{i}')
            inst = y.dma_start(out=out_d[i * 128:(i + 1) * 128, :], in_=xt(i)) if y else None
            r.mark('lst', f'dbgst{i}', inst, 16)
        r.wait('y', f'dbgst{NT - 1}')

    if DBG == 'P1':
        dbg_dump_x('L1xadd')
        return

    def slice_load(h):
        if h == 0:
            r.wait('y', 'P2sp10')
            r.wait('y', 'P2sp11')
        sem = 'lslA' if h % 2 == 0 else 'lslB'
        inst = y.dma_start(out=QTs[h % 2][0:64, :], in_=qt_d[64 * h:64 * h + 64, :]) if y else None
        r.mark(sem, f'sl{h}q', inst, 16)
        inst = y.dma_start(out=KTs[h % 2][0:64, :], in_=kt_d[64 * h:64 * h + 64, :]) if y else None
        r.mark(sem, f'sl{h}k', inst, 16)

    layernorm('L2', lambda i: f'L1xadd{i}')
    transpose_lnx('L2')
    for pi, (wt, bcv, dst_d, bev, wev) in enumerate((
            (Wqt, bqc, qt_d, 'bq', 'wq7'), (Wkt, bkc, kt_d, 'bk', 'wk7'),
            (Wvt, bvc, vt_d, 'bv', 'wv7'))):
        for m in range(4):
            idx = pi * 4 + m
            which = idx % 2
            psrc = pzBig if which == 0 else pzB2
            if idx >= 2:
                r.wait('t', f'P2ev{idx - 2}')
            if idx == 0:
                r.wait('t', 'L1xadd15')
            r.wait('t', 'woutd')
            for c in range(4):
                for k in range(8):
                    r.wait('t', f'L2tpe{k * 2 + (c // 2)}')
                    inst = t.matmul(psrc[:, c * 512:(c + 1) * 512],
                                    wt[k][:, m * 128:(m + 1) * 128],
                                    LNxT[k][:, c * 512:(c + 1) * 512],
                                    start=(k == 0), stop=(k == 7)) if t else None
            r.mark('tm', f'P2mm{idx}', inst)
            r.wait('s', f'P2mm{idx}')
            r.wait('s', 'bout')
            if idx >= 2:
                r.wait('s', f'P2sp{idx - 2}')
            with LP():
                inst = sc.activation(out=stg[which][:, :], in_=psrc[:, :], func=AF.Identity,
                                     bias=bcv[:, m:m + 1]) if sc else None
            r.mark('sm', f'P2ev{idx}', inst)
            r.wait('y', f'P2ev{idx}')
            inst = y.dma_start(out=dst_d[m * 128:(m + 1) * 128, :], in_=stg[which][:, :]) if y else None
            r.mark('lspA' if which == 0 else 'lspB', f'P2sp{idx}', inst, 16)
    slice_load(0)

    for i in range(NT):
        r.wait('y', f'L2ln{i}')
        inst = y.dma_start(out=xs_d[i * 128:(i + 1) * 128, :], in_=xt(i)) if y else None
        r.mark('lxs', f'xsp{i}', inst, 16)

    if DBG == 'QT':
        r.wait('g', 'P2sp10')
        r.wait('g', 'P2sp11')
        inst = g.dma_start(out=out_d[0:1024, :],
                           in_=qt_d.rearrange("a (b c) -> (a b) c", c=1024)) if g else None
        r.mark('lst', 'dbgqt', inst, 16)
        inst = g.dma_start(out=out_d[1024:2048, :],
                           in_=kt_d.rearrange("a (b c) -> (a b) c", c=1024)) if g else None
        r.mark('lst', 'dbgkt', inst, 16)
        r.wait('g', 'dbgkt')
        return



    for h in range(H):
        QTh = QTs[h % 2][0:64, :]
        KTh = KTs[h % 2][0:64, :]
        if h > 0:
            r.wait('y', f'h{h - 1}vtp')
        inst = y.dma_start(out=VTs[0:64, :], in_=vt_d[64 * h:64 * h + 64, :]) if y else None
        r.mark('lvt', f'vts{h}', inst, 16)

        for i in range(NT):
            idx = i
            which = idx % 2
            psrc = pzBig if which == 0 else pzB2
            if idx == 0:
                r.wait('t', f'sl{h}k')
                if h == 0:
                    r.wait('t', 'P2ev10')
                    r.wait('t', 'P2ev11')
                else:
                    r.wait('t', f'h{h - 1}ubce3')
            if idx == 1 and h > 0:
                r.wait('t', f'h{h - 1}aslc')
            if idx >= 2:
                r.wait('t', f'h{h}ex{idx - 2}')
            for c in range(4):
                inst = t.matmul(psrc[:, c * 512:(c + 1) * 512],
                                QTh[:, i * 128:(i + 1) * 128],
                                KTh[:, c * 512:(c + 1) * 512],
                                start=True, stop=True) if t else None
            r.mark('tm', f'h{h}lg{idx}', inst)
            r.wait('s', f'h{h}lg{idx}')
            if h == 0 and idx == 8:
                r.wait('s', 'xsp15')
            if idx == 0 and h > 0:
                r.wait('s', f'h{h - 1}p{NS - 1}')
            with LP():
                inst = sc.activation(out=Atile[i], in_=psrc[:, :], func=AF.Exp,
                                     scale=0.125) if sc else None
            r.mark('sm', f'h{h}ex{idx}', inst)
        if h + 1 < H:
            if h >= 1:
                r.wait('y', f'h{h - 1}lg{NT - 1}')
            slice_load(h + 1)
        r.wait('t', f'h{h}ex{NT - 1}')
        r.wait('t', f'h{h}ex{NT - 2}')
        for ti in range(NT):
            for half in range(2):
                if ti > 0:
                    r.wait('t', f'h{h}bev{ti - 1}_{half}')
                elif h > 0:
                    r.wait('t', f'h{h - 1}gve{half}' if False else f'h{h - 1}aslc')
                for j in range(8 * half, 8 * half + 8):
                    inst = t.transpose(pb[half][:, (j - 8 * half) * 128:(j - 8 * half + 1) * 128],
                                       Atile[j][:, ti * 128:(ti + 1) * 128], ident[:, :]) if t else None
                r.mark('tm', f'h{h}btp{ti}_{half}', inst)
                r.wait('v', f'h{h}btp{ti}_{half}')
                with LP():
                    inst = v.tensor_copy(out=Btile[ti][:, half * 1024:(half + 1) * 1024],
                                         in_=pb[half][:, :]) if v else None
                r.mark('vm', f'h{h}bev{ti}_{half}', inst)

        inst = v.memset(gcol16[:, :], 1.0) if v else None
        r.mark('vm', f'h{h}rinit', inst)
        for s in range(NS):
            mat = Btile if s % 2 == 0 else Atile
            vecap = gcol16 if s % 2 == 0 else ucol16
            r.wait('t', f'h{h}r{s - 1}' if s > 0 else f'h{h}rinit')
            if s > 0:
                r.wait('t', f'h{h}evs{s - 1}_1')
                r.wait('t', f'h{h}evv{s - 1}_1')
            for ti in range(NT):
                if s == 0:
                    r.wait('t', f'h{h}bev{ti}_0')
                    r.wait('t', f'h{h}bev{ti}_1')
                for c in range(4):
                    inst = t.matmul(pz[c][32 * c:32 * c + 1, :], vecap[:, ti:ti + 1],
                                    mat[ti][:, c * 512:(c + 1) * 512],
                                    start=(ti == 0), stop=(ti == NT - 1),
                                    tile_position=(0, 32 * c)) if t else None
            r.mark('tm', f'h{h}p{s}', inst)
            for w in range(4):
                if t:
                    t.matmul(pwarm[0:1, :], gcol16[:, 0:1],
                             Btile[0][:, (w % 4) * 512:(w % 4 + 1) * 512],
                             start=True, stop=True, tile_position=(0, 0))
            if True:
                for gq in range(4):
                    if gq < 2:
                        r.wait('t', f'h{h}evs{s}_{gq}')
                    else:
                        r.wait('t', f'h{h}evv{s}_{gq - 2}')
                    if gq == 0 and s > 0:
                        r.wait('t', f'h{h}r{s - 1}')
                    p = 32 * gq
                    for j in range(4 * gq, 4 * gq + 4):
                        inst = t.matmul(pt[:, j:j + 1],
                                        yrow[p:p + 1, j * 128:(j + 1) * 128],
                                        ones16[p:p + 1, 0:1],
                                        start=True, stop=True, tile_position=(p, 0)) if t else None
                    r.mark('tm', f'h{h}tp{s}_{gq}', inst)
                for w in range(3):
                    if t:
                        t.matmul(pwarm[0:1, :], gcol16[:, 0:1],
                                 Btile[1][:, (w % 4) * 512:(w % 4 + 1) * 512],
                                 start=True, stop=True, tile_position=(0, 0))
            r.wait('s', f'h{h}p{s}')
            for c in range(2):
                with LP():
                    inst = sc.copy(out=yrow[32 * c:32 * c + 1, c * 512:(c + 1) * 512],
                                   in_=pz[c][32 * c:32 * c + 1, :]) if sc else None
                r.mark('sm', f'h{h}evs{s}_{c}', inst)
            r.wait('v', f'h{h}p{s}')
            for c in range(2, 4):
                with LP():
                    inst = v.tensor_copy(out=yrow[32 * c:32 * c + 1, c * 512:(c + 1) * 512],
                                         in_=pz[c][32 * c:32 * c + 1, :]) if v else None
                r.mark('vm', f'h{h}evv{s}_{c - 2}', inst)
            for gq in range(4):
                r.wait('v', f'h{h}tp{s}_{gq}')
            if s == NS - 1:
                inst = v.reciprocal(out=gcol32[:, :], in_=pt[:, :]) if v else None
            else:
                col16 = ucol16 if s % 2 == 0 else gcol16
                with LP():
                    inst = v.reciprocal(out=col16[:, :], in_=pt[:, :]) if v else None
            r.mark('vm', f'h{h}r{s}', inst)
            if s == 2:
                r.wait('t', f'vts{h}')
                if h > 0:
                    r.wait('t', f'h{h - 1}vke')
                for j in range(16):
                    inst = t.transpose(pb[0][:, j * 64:(j + 1) * 64],
                                       VTs[0:64, j * 128:(j + 1) * 128], ident[0:64, 0:64]) if t else None
                r.mark('tm', f'h{h}vtp', inst)
                r.wait('v', f'h{h}vtp')
                with LP():
                    inst = v.tensor_copy(out=Vk[:, :], in_=pb[0][:, 0:1024]) if v else None
                r.mark('vm', f'h{h}vke', inst)
            if s == NS - 1:
                r.wait('t', f'h{h}vke')
                for j in range(16):
                    dst = pb[0] if j < 8 else pb[1]
                    inst = t.transpose(dst[0:1, (j % 8) * 128:(j % 8 + 1) * 128],
                                       ucol16[:, j:j + 1], ident[:, :]) if t else None
                r.mark('tm', f'h{h}utp', inst)
                r.wait('s', f'h{h}utp')
                for half in range(2):
                    with LP():
                        inst = sc.copy(out=urow16[0:1, half * 1024:(half + 1) * 1024],
                                       in_=(pb[0] if half == 0 else pb[1])[0:1, :]) if sc else None
                r.mark('sm', f'h{h}uev', inst)
                r.wait('t', f'h{h}uev')
                r.wait('t', f'h{h}r{NS - 1}')
                r.wait('t', f'h{h}evs{NS - 1}_1')
                r.wait('t', f'h{h}evv{NS - 1}_1')
                for c in range(4):
                    inst = t.matmul(pz[c][0:64, :], ones16[0:1, 0:64],
                                    urow16[0:1, c * 512:(c + 1) * 512],
                                    start=True, stop=True, tile_position=(0, 0)) if t else None
                    r.mark('tm', f'h{h}ubc{c}', inst)
                    r.wait('v', f'h{h}ubc{c}')
                    with LP():
                        inst = v.tensor_copy(out=ubc16[0:64, c * 512:(c + 1) * 512],
                                             in_=pz[c][0:64, :]) if v else None
                    r.mark('vm', f'h{h}ubce{c}', inst)

        r.wait('v', f'h{h}r{NS - 1}')
        r.wait('v', f'h{h}vke')
        for j in range(16):
            with LP():
                inst = v.tensor_scalar_mul(gV[:, j * 64:(j + 1) * 64],
                                           Vk[:, j * 64:(j + 1) * 64],
                                           gcol32[:, j:j + 1]) if v else None
        r.mark('vm', f'h{h}gvs', inst)
        r.wait('t', f'h{h}gvs')
        r.wait('t', f'h{h}r{NS - 1}_3' if False else f'h{h}uev')
        for c in range(4):
            for ti in range(NT):
                inst = t.matmul(pzc[c][0:64, :], gV[:, ti * 64:(ti + 1) * 64],
                                Btile[ti][:, c * 512:(c + 1) * 512],
                                start=(ti == 0), stop=(ti == NT - 1)) if t else None
        r.mark('tm', f'h{h}avmm', inst)
        r.wait('v', f'h{h}avmm')
        for c in range(4):
            r.wait('v', f'h{h}ubce{c}')
        if h > 0:
            r.wait('v', f'h{h - 1}asp')
        with LP():
            inst = v.tensor_mul(aslc[0:64, :], pzB2[0:64, :], ubc16[0:64, :]) if v else None
        r.mark('vm', f'h{h}aslc', inst)
        r.wait('y', f'h{h}aslc')
        inst = y.dma_start(out=at_d[64 * h:64 * h + 64, :], in_=aslc[0:64, :]) if y else None
        r.mark('lat', f'h{h}asp', inst, 16)

    if DBG == 'H0':
        r.wait('g', 'h0asp')
        inst = g.dma_start(out=out_d[0:128, :],
                           in_=at_d[0:64, :].rearrange("a (b c) -> (a b) c", c=1024)) if g else None
        r.mark('lst', 'dbgh0', inst, 16)
        r.wait('g', 'dbgh0')
        return

    r.wait('y', f'h{H - 1}asp')
    for j in range(4):
        inst = y.dma_start(out=atT[j], in_=at_d[j * 128:(j + 1) * 128, :]) if y else None
        r.mark('lat', f'atl{j}', inst, 16)
    for i in range(NT):
        inst = y.dma_start(out=xt(i), in_=xs_d[i * 128:(i + 1) * 128, :]) if y else None
        r.mark('ldx', f'xrl{i}', inst, 16)
    r.wait('g', f'h{H - 1}aslc')
    for j in range(4):
        inst = g.dma_start(out=Wot[j], in_=wo_d[j * 128:(j + 1) * 128, :]) if g else None
        r.mark('ldw', f'wo{j}', inst, 16)
    inst = g.dma_start(out=Woutb2[0:64, :], in_=wout_d[:, :]) if g else None
    r.mark('ldw', 'wout2', inst, 16)
    inst = g.dma_start(out=Woutb2[64:128, :], in_=wout_d[:, :]) if g else None
    r.mark('ldw', 'wout2d', inst, 16)
    for k in range(8):
        inst = g.dma_start(out=Winb2[k], in_=win_d[k * 128:(k + 1) * 128, :]) if g else None
        r.mark('ldw', f'win2_{k}', inst, 16)

    for i in range(NT):
        which = i % 2
        if i >= 2:
            r.wait('t', f'P4xa{i - 2}')
        if i == 0:
            r.wait('t', 'atl3')
            r.wait('t', 'win2_7')
            r.wait('t', 'bo16')
            r.wait('t', f'h{H - 1}aslc')
        for nch in range(2):
            for j in range(4):
                inst = t.matmul(pzh[which][:, nch * 512:(nch + 1) * 512],
                                atT[j][:, i * 128:(i + 1) * 128],
                                Wot[j][:, nch * 512:(nch + 1) * 512],
                                start=(j == 0), stop=False) if t else None
            inst = t.matmul(pzh[which][:, nch * 512:(nch + 1) * 512],
                            ones16[0:1, 0:128], bo16[0:1, nch * 512:(nch + 1) * 512],
                            start=False, stop=True, tile_position=(0, 0)) if t else None
        r.mark('tm', f'P4mm{i}', inst)
        r.wait('v', f'P4mm{i}')
        r.wait('v', 'xrl15')
        inst = v.tensor_add(xt(i), xt(i), pzh[which][:, :]) if v else None
        r.mark('vm', f'P4xa{i}', inst)

    if DBG == 'P4':
        dbg_dump_x('P4xa')
        return

    layernorm('L5', lambda i: f'P4xa{i}', guard='P4mm15')
    transpose_lnx('L5')
    ffn('L5', Winb2, Woutb2, 'win2_7', 'win2_7', pzfree='P4xa15')

    for i in range(NT):
        r.wait('y', f'L5xadd{i}')
        inst = y.dma_start(out=out_d[i * 128:(i + 1) * 128, :], in_=xt(i)) if y else None
        r.mark('lst', f'st{i}', inst, 16)
    r.wait('y', f'st{NT - 1}')


def build_nc():
    nc = bass.Bass("TRN2", target_bir_lowering=False, debug=False)
    names = [("x_in", [T, D], F32, "ExternalInput"),
             ("wq", [D, 512], F32, "ExternalInput"),
             ("bq", [4, 128], F32, "ExternalInput"),
             ("wk", [D, 512], F32, "ExternalInput"),
             ("bk", [4, 128], F32, "ExternalInput"),
             ("wv", [D, 512], F32, "ExternalInput"),
             ("bv", [4, 128], F32, "ExternalInput"),
             ("wo", [512, D], F32, "ExternalInput"),
             ("bo", [D], F32, "ExternalInput"),
             ("win", [D, 64], F32, "ExternalInput"),
             ("binp", [128, 1], F32, "ExternalInput"),
             ("w1", [128, 16], F32, "ExternalInput"),
             ("b1", [128, 16], F32, "ExternalInput"),
             ("w2", [128, 16], F32, "ExternalInput"),
             ("b2", [64, 1], F32, "ExternalInput"),
             ("wout", [64, D], F32, "ExternalInput"),
             ("bout", [D], F32, "ExternalInput"),
             ("identb", [128, 128], BF16, "ExternalInput"),
             ("out", [T, D], F32, "ExternalOutput"),
             ("xs", [T, D], F32, "Internal"),
             ("qts", [512, T], BF16, "Internal"),
             ("kts", [512, T], BF16, "Internal"),
             ("vts", [512, T], BF16, "Internal"),
             ("ats", [512, T], BF16, "Internal")]
    io = [nc.dram_tensor(n, s, d, kind=k).ap() for n, s, d, k in names]

    with contextlib.ExitStack() as stack:
        e = stack.enter_context
        A = e(nc.sbuf_tensor("Areg", [128, 32768], BF16))
        Bm = e(nc.sbuf_tensor("Breg", [128, 32768], BF16))
        Q = e(nc.sbuf_tensor("Qreg", [128, 30208], BF16))
        M = e(nc.sbuf_tensor("Mreg", [128, 9216], BF16))
        pzBig = e(nc.psum_tensor("pzBig", [128, 2048], F32))
        pzB2 = e(nc.psum_tensor("pzB2", [128, 2048], F32))
        sems = {nm: e(nc.semaphore(nm)) for nm in
                ('ldx', 'ldx2', 'lxs', 'ldsm', 'ldw', 'lspA', 'lspB', 'lslA', 'lslB',
                 'lvt', 'lat', 'lst', 'tm', 'sm', 'vm')}
        block = e(nc.Block())
        bufs = (A[:, :], Bm[:, :], Q[:, :], M[:, :], pzBig[:, :], pzB2[:, :])

        planner = Run()
        body(planner, nc, io, bufs)
        plan = planner.plan

        @block.tensor
        def _(E):
            body(Run('t', plan, sems, E), nc, io, bufs)

        @block.scalar
        def _(E):
            body(Run('s', plan, sems, E), nc, io, bufs)

        @block.vector
        def _(E):
            body(Run('v', plan, sems, E), nc, io, bufs)

        @block.sync
        def _(E):
            body(Run('y', plan, sems, E), nc, io, bufs)

        @block.gpsimd
        def _(E):
            body(Run('g', plan, sems, E), nc, io, bufs)

    return nc


def kernel(L0_attn_x, L0_attn_Wq, L0_attn_bq, L0_attn_Wk, L0_attn_bk,
           L0_attn_Wv, L0_attn_bv, L0_attn_Wo, L0_attn_bo,
           L0_ffn_Win, L0_ffn_bin_, L0_ffn_W1, L0_ffn_b1, L0_ffn_W2,
           L0_ffn_b2, L0_ffn_Wout, L0_ffn_bout):
    x = np.ascontiguousarray(np.asarray(L0_attn_x, dtype=np.float32))
    common = {
        "wq": np.ascontiguousarray(np.asarray(L0_attn_Wq, np.float32)),
        "bq": np.ascontiguousarray(np.asarray(L0_attn_bq, np.float32).reshape(4, 128)),
        "wk": np.ascontiguousarray(np.asarray(L0_attn_Wk, np.float32)),
        "bk": np.ascontiguousarray(np.asarray(L0_attn_bk, np.float32).reshape(4, 128)),
        "wv": np.ascontiguousarray(np.asarray(L0_attn_Wv, np.float32)),
        "bv": np.ascontiguousarray(np.asarray(L0_attn_bv, np.float32).reshape(4, 128)),
        "wo": np.ascontiguousarray(np.asarray(L0_attn_Wo, np.float32)),
        "bo": np.ascontiguousarray(np.asarray(L0_attn_bo, np.float32)),
        "win": np.ascontiguousarray(np.asarray(L0_ffn_Win, np.float32)),
        "binp": np.ascontiguousarray(np.tile(np.asarray(L0_ffn_bin_, np.float32).reshape(64, 1), (2, 1))),
        "w1": np.ascontiguousarray(np.tile(np.asarray(L0_ffn_W1, np.float32), (2, 1))),
        "b1": np.ascontiguousarray(np.tile(np.asarray(L0_ffn_b1, np.float32), (2, 1))),
        "w2": np.ascontiguousarray(np.tile(np.asarray(L0_ffn_W2, np.float32), (2, 1))),
        "b2": np.ascontiguousarray(np.asarray(L0_ffn_b2, np.float32).reshape(64, 1)),
        "wout": np.ascontiguousarray(np.asarray(L0_ffn_Wout, np.float32)),
        "bout": np.ascontiguousarray(np.asarray(L0_ffn_bout, np.float32)),
        "identb": np.eye(128, dtype=ml_dtypes.bfloat16),
    }
    nc = build_nc()
    in_maps = [dict(common, x_in=np.ascontiguousarray(x[b])) for b in range(8)]
    res = run_bass_kernel_spmd(nc, in_maps, core_ids=list(range(8)))
    out = np.stack([np.asarray(res.results[b]["out"]) for b in range(8)], axis=0)
    return out.astype(np.float32)


# version 20
# speedup vs baseline: 1.2051x; 1.1927x over previous
import contextlib
import numpy as np
import ml_dtypes
import concourse.bass as bass
import concourse.mybir as mybir
from concourse.bass_utils import run_bass_kernel_spmd

F32 = mybir.dt.float32
BF16 = mybir.dt.bfloat16
AF = mybir.ActivationFunctionType
OP = mybir.AluOpType
AX = mybir.AxisListType.X

DBG = None
T = 2048
D = 1024
H = 8
NT = 16
ITERS = 20
NS = 2 * ITERS
NWARM = 6


class Run:
    def __init__(self, eng=None, plan=None, sems=None, E=None):
        self.eng = eng
        self.planning = plan is None
        self.plan = {} if plan is None else plan
        self.counts = {}
        self.sems = sems
        self.E = E

    def mark(self, sem, event, inst, n=1):
        if self.planning:
            c = self.counts.get(sem, 0) + n
            self.counts[sem] = c
            assert event not in self.plan, f"dup event {event}"
            self.plan[event] = (sem, c)
        if inst is not None:
            inst.then_inc(self.sems[sem], n)

    def wait(self, eng, event):
        if self.planning or eng != self.eng:
            return
        sem, c = self.plan[event]
        self.E.wait_ge(self.sems[sem], c)

    def own(self, eng):
        return (not self.planning) and eng == self.eng


def body(r, nc, io, bufs):
    t = r.E if r.own('t') else None
    sc = r.E if r.own('s') else None
    v = r.E if r.own('v') else None
    y = r.E if r.own('y') else None
    g = r.E if r.own('g') else None

    A, Bm, Q, M, pzBig, pzB2 = bufs

    Af32 = A.bitcast(F32)
    Bf32 = Bm.bitcast(F32)
    LNx = [A[:, i * 1024:(i + 1) * 1024] for i in range(NT)]
    LNxT = [Bm[:, j * T:(j + 1) * T] for j in range(8)]
    Atile = [A[:, i * T:(i + 1) * T] for i in range(NT)]
    Btile = [Bm[:, i * T:(i + 1) * T] for i in range(NT)]
    atT = [A[:, j * T:(j + 1) * T] for j in range(4)]

    def xt(i):
        if i < 8:
            return Af32[:, 8192 + i * 1024: 8192 + (i + 1) * 1024]
        return Bf32[:, 8192 + (i - 8) * 1024: 8192 + (i - 7) * 1024]

    s2 = Af32[:, 0:1024]
    hb = [Af32[:, 1024:2048], Af32[:, 2048:3072]]
    gA = Af32[:, 3072:4096]
    gB = Af32[:, 4096:5120]
    bout_r = Af32[0:1, 5120:6144]
    hbe_r = Af32[0:1, 6144:7168]
    tmp_r = Af32[0:1, 7168:8192]
    bo_r = Bf32[0:1, 512:1536]
    Woutb0 = Q[:, 8192:9216]

    stg = [Q[:, 0:2048], Q[:, 2048:4096]]
    QTs = [Q[:, 0:2048], Q[:, 2048:4096]]
    KTs = [Q[:, 4096:6144], Q[:, 6144:8192]]
    VTs = Q[:, 8192:10240]
    Vk = Q[:, 10240:11264]
    gV = Q[:, 14336:15360]
    aslc = Q[:, 15360:17408]
    ubc16 = Q[:, 12288:14336]
    urow16 = Q[:, 21504:23552]
    Wqt = [Q[:, 17408 + k * 512:17408 + (k + 1) * 512] for k in range(8)]
    Wkt = [Q[:, 21504 + k * 512:21504 + (k + 1) * 512] for k in range(8)]
    Wvt = [Q[:, 25600 + k * 512:25600 + (k + 1) * 512] for k in range(8)]
    Wint = [Q[:, 29696 + k * 64:29696 + (k + 1) * 64] for k in range(8)]
    ubc32 = Q[:, 17408:21504].bitcast(F32)
    Wot = [Q[:, j * 1024:(j + 1) * 1024] for j in range(4)]
    Woutb2 = Q[:, 4096:5120]
    Winb2 = [Q[:, 5120 + k * 64:5120 + (k + 1) * 64] for k in range(8)]

    yrow = M[:, 0:2048]
    ident = M[:, 2048:2176]
    ones16 = M[:, 2176:2304]
    ucol16 = M[:, 2304:2320]
    gcol16 = M[:, 2320:2336]
    bo16 = M[:, 2336:3360]
    hbe16 = M[:, 3360:4384]
    b2c16 = M[:, 4384:4385]
    Mf = M[:, 4608:9216].bitcast(F32)
    ones32 = Mf[:, 0:128]
    gcol32 = Mf[:, 128:144]
    bnst = Mf[:, 144:156]
    bnagg = Mf[:, 160:192]
    mean = Mf[:, 1152:1168]
    var = Mf[:, 1168:1184]
    rstd = Mf[:, 1184:1200]
    nms = Mf[:, 1200:1216]
    msq = Mf[:, 1216:1232]
    bqc = Mf[:, 1232:1236]
    bkc = Mf[:, 1236:1240]
    bvc = Mf[:, 1240:1244]
    binc = Mf[:, 1244:1245]
    b2c = Mf[:, 1245:1246]
    w1c = Mf[:, 1248:1264]
    b1c = Mf[:, 1264:1280]
    w2h = Mf[:, 1280:1296]
    w2r = Mf[:, 1296:1312]

    pz = [pzBig[:, c * 512:(c + 1) * 512] for c in range(4)]
    pzh = [pzBig[:, 0:1024], pzBig[:, 1024:2048]]
    pt = pzB2[:, 0:16]
    pwarm = pzB2[:, 512:1024]
    pb = [pzB2[:, 1024:1536].bitcast(BF16), pzB2[:, 1536:2048].bitcast(BF16)]
    pzc = [pzB2[:, c * 512:(c + 1) * 512] for c in range(4)]

    (x_d, wq_d, bq_d, wk_d, bk_d, wv_d, bv_d, wo_d, bo_d,
     win_d, bin_d, w1_d, b1_d, w2_d, b2_d, wout_d, bout_d,
     ident_d, out_d, xs_d, qt_d, kt_d, vt_d, at_d) = io

    LP = lambda: nc.allow_low_precision(reason="bf16 matmul-operand staging (blessed)")

    for i in range(NT):
        inst = y.dma_start(out=xt(i), in_=x_d[i * 128:(i + 1) * 128, :]) if y else None
        r.mark('ldx' if i < 8 else 'ldx2', f'x{i}', inst, 16)
    inst = y.dma_start(out=ident[:, :], in_=ident_d[:, :]) if y else None
    r.mark('ldsm', 'ident', inst, 16)
    for nm, dst, src in (('bq', bqc, bq_d), ('bk', bkc, bk_d), ('bv', bvc, bv_d)):
        inst = y.dma_start(out=dst[:, :], in_=src.rearrange("m p -> p m"),
                           allow_slow_non_contiguous=True) if y else None
        r.mark('ldsm', nm, inst, 16)
    inst = y.dma_start(out=binc[:, :], in_=bin_d[:, :]) if y else None
    r.mark('ldsm', 'bin0', inst, 16)
    inst = y.dma_start(out=b2c[0:64, :], in_=b2_d[0:64, :]) if y else None
    r.mark('ldsm', 'b2', inst, 16)
    for nm, dst, src in (('w1', w1c, w1_d), ('b1', b1c, b1_d), ('w2', w2r, w2_d)):
        inst = y.dma_start(out=dst[:, :], in_=src[:, :]) if y else None
        r.mark('ldsm', f'{nm}0', inst, 16)
    inst = y.dma_start(out=bo_r, in_=bo_d.rearrange("(a d) -> a d", a=1)) if y else None
    r.mark('ldsm', 'bo', inst, 16)
    inst = y.dma_start(out=bout_r, in_=bout_d.rearrange("(a d) -> a d", a=1)) if y else None
    r.mark('ldsm', 'bout', inst, 16)
    for k in range(8):
        inst = g.dma_start(out=Wqt[k], in_=wq_d[k * 128:(k + 1) * 128, :]) if g else None
        r.mark('ldw', f'wq{k}', inst, 16)
        inst = g.dma_start(out=Wkt[k], in_=wk_d[k * 128:(k + 1) * 128, :]) if g else None
        r.mark('ldw', f'wk{k}', inst, 16)
        inst = g.dma_start(out=Wvt[k], in_=wv_d[k * 128:(k + 1) * 128, :]) if g else None
        r.mark('ldw', f'wv{k}', inst, 16)
        inst = g.dma_start(out=Wint[k], in_=win_d[k * 128:(k + 1) * 128, :]) if g else None
        r.mark('ldw', f'win{k}', inst, 16)
    inst = g.dma_start(out=Woutb0[0:64, :], in_=wout_d[:, :]) if g else None
    r.mark('ldw', 'wout', inst, 16)
    inst = g.dma_start(out=Woutb0[64:128, :], in_=wout_d[:, :]) if g else None
    r.mark('ldw', 'woutd', inst, 16)

    if v:
        v.memset(ones16[:, :], 1.0)
        v.memset(ones32[:, :], 1.0)
    inst = v.memset(yrow[:, :], 0.0) if v else None
    r.mark('vm', 'consts', inst)
    r.wait('v', 'bout')
    inst = v.tensor_scalar_mul(w2h[:, :], w2r[:, :], 0.5) if v else None
    r.mark('vm', 'w2half', inst)

    r.wait('v', 'bout')
    with LP():
        inst = v.tensor_copy(out=b2c16[0:64, :], in_=b2c[0:64, :]) if v else None
    r.mark('vm', 'b2c16', inst)
    r.wait('t', 'b2c16')
    r.wait('t', 'woutd')
    for c in range(2):
        inst = t.matmul(pzc[c][0:1, :], b2c16[0:64, 0:1], Woutb0[0:64, c * 512:(c + 1) * 512],
                        start=True, stop=True) if t else None
        r.mark('tm', f'wtb2_{c}', inst)
    r.wait('s', 'bout')
    inst = sc.mul(tmp_r, bout_r, 0.5) if sc else None
    r.mark('sm', 'halfbout', inst)
    r.wait('v', 'wtb2_0')
    r.wait('v', 'wtb2_1')
    r.wait('v', 'halfbout')
    for c in range(2):
        with LP():
            inst = v.scalar_tensor_tensor(out=hbe_r[:, c * 512:(c + 1) * 512],
                                          in0=pzc[c][0:1, :], scalar=0.5,
                                          in1=tmp_r[:, c * 512:(c + 1) * 512],
                                          op0=OP.mult, op1=OP.add) if v else None
        r.mark('vm', f'hbe{c}', inst)
    r.wait('s', 'hbe0')
    r.wait('s', 'hbe1')
    with LP():
        inst = sc.copy(out=hbe16[0:1, :], in_=hbe_r) if sc else None
    r.mark('sm', 'hbe16', inst)
    r.wait('s', 'bout')
    with LP():
        inst = sc.copy(out=bo16[0:1, :], in_=bo_r) if sc else None
    r.mark('sm', 'bo16', inst)

    def warm(n):
        if t:
            for w in range(n):
                t.matmul(pwarm[0:1, :], ones16[:, 0:1], yrow[:, (w % 3) * 512:(w % 3) * 512 + 512],
                         start=True, stop=True, tile_position=(0, 0))

    def layernorm(tag, xev, guard=None):
        meanv = bnagg[:, 0:32:2]
        varv = bnagg[:, 1:32:2]
        for i in range(NT):
            if i == 0 and guard is not None:
                r.wait('v', guard)
                r.wait('s', guard)
            r.wait('v', xev(i))
            if i > 0:
                r.wait('v', f'{tag}ag{i - 1}')
            inst = v.bn_stats(bnst[:, 0:6], xt(i)[:, 0:512]) if v else None
            r.mark('vm', f'{tag}ba{i}', inst)
            r.wait('v', f'{tag}ba{i}')
            inst = v.bn_stats(bnst[:, 6:12], xt(i)[:, 512:1024]) if v else None
            r.mark('vm', f'{tag}bb{i}', inst)
            r.wait('v', f'{tag}bb{i}')
            inst = v.bn_aggr(bnagg[:, 2 * i:2 * i + 2], bnst[:, 0:12]) if v else None
            r.mark('vm', f'{tag}ag{i}', inst)
        r.wait('v', f'{tag}ag{NT - 1}')
        inst = v.tensor_scalar_add(var[:, :], varv, 1e-5) if v else None
        r.mark('vm', f'{tag}m5', inst)
        r.wait('s', f'{tag}m5')
        inst = sc.sqrt(msq[:, :], var[:, :]) if sc else None
        r.mark('sm', f'{tag}m6', inst)
        r.wait('v', f'{tag}m6')
        inst = v.reciprocal(rstd[:, :], msq[:, :]) if v else None
        r.mark('vm', f'{tag}m7', inst)
        r.wait('v', f'{tag}m7')
        with LP():
            inst = v.scalar_tensor_tensor(out=nms[:, :], in0=meanv, scalar=-1.0,
                                          in1=rstd[:, :], op0=OP.mult, op1=OP.mult) if v else None
        r.mark('vm', f'{tag}m8', inst)
        for i in range(NT):
            r.wait('s', f'{tag}m8')
            with LP():
                inst = sc.activation(out=LNx[i], in_=xt(i), func=AF.Identity,
                                     scale=rstd[:, i:i + 1], bias=nms[:, i:i + 1]) if sc else None
            r.mark('sm', f'{tag}ln{i}', inst)

    def transpose_lnx(tag):
        r.wait('t', 'bout')
        for j in range(8):
            for half in range(2):
                idx = j * 2 + half
                which = idx % 2
                if idx >= 2:
                    r.wait('t', f'{tag}tpe{idx - 2}')
                for i in range(8 * half, 8 * half + 8):
                    r.wait('t', f'{tag}ln{i}')
                    inst = t.transpose(pb[which][:, (i - 8 * half) * 128:(i - 8 * half + 1) * 128],
                                       LNx[i][:, j * 128:(j + 1) * 128], ident[:, :]) if t else None
                r.mark('tm', f'{tag}tp{idx}', inst)
                r.wait('v', f'{tag}tp{idx}')
                inst = v.tensor_copy(out=LNxT[j][:, half * 1024:(half + 1) * 1024],
                                     in_=pb[which][:, :]) if v else None
                r.mark('vm', f'{tag}tpe{idx}', inst)

    def ffn(tag, win_tiles, woutb, win_ev, wout_ev, pzfree=None):
        for c in range(4):
            r.wait('t', win_ev)
            if c == 0 and pzfree is not None:
                r.wait('t', pzfree)
            for k in range(8):
                r.wait('t', f'{tag}tpe{k * 2 + (c // 2)}')
                inst = t.matmul(pz[c][0:64, :], win_tiles[k][:, :],
                                LNxT[k][:, c * 512:(c + 1) * 512],
                                start=(k == 0), stop=(k == 7)) if t else None
            r.mark('tm', f'{tag}sin{c}', inst)
        for c in range(4):
            r.wait('s', f'{tag}sin{c}')
            r.wait('s', 'bout')
            half, col = c % 2, (c // 2) * 512
            inst = sc.activation(out=s2[64 * half:64 * half + 64, col:col + 512],
                                 in_=pz[c][0:64, :], func=AF.Identity,
                                 bias=binc[64 * half:64 * half + 64, :]) if sc else None
            r.mark('sm', f'{tag}s2_{c}', inst)
        for i in range(16):
            for c in range(4):
                r.wait('s', f'{tag}s2_{c}')
            r.wait('s', 'bout')
            if i >= 2:
                r.wait('s', f'{tag}g{i - 2}')
            inst = sc.activation(out=hb[i % 2], in_=s2[:, :], func=AF.Relu,
                                 scale=w1c[:, i:i + 1], bias=b1c[:, i:i + 1]) if sc else None
            r.mark('sm', f'{tag}h{i}', inst)
            r.wait('v', f'{tag}h{i}')
            gsrc, gdst = (gA, gB) if i % 2 else (gB, gA)
            if i == 0:
                r.wait('v', 'w2half')
                inst = v.tensor_scalar_mul(gdst, hb[i % 2], w2h[:, 0:1]) if v else None
            else:
                r.wait('v', f'{tag}g{i - 1}')
                with LP():
                    inst = v.scalar_tensor_tensor(out=gdst, in0=hb[i % 2], scalar=w2h[:, i:i + 1],
                                                  in1=gsrc, op0=OP.mult, op1=OP.add) if v else None
            r.mark('vm', f'{tag}g{i}', inst)
        gfin = gB
        if DBG == 'S2' and tag == 'L1':
            for c in range(4):
                r.wait('y', f'{tag}s2_{c}')
            inst = y.dma_start(out=out_d[0:128, :], in_=s2) if y else None
            r.mark('lst', 'dbgs2', inst, 16)
            r.wait('v', f'{tag}g15')
            r.wait('y', f'{tag}g15')
            inst = y.dma_start(out=out_d[128:256, :], in_=gfin) if y else None
            r.mark('lst', 'dbgg', inst, 16)
            r.wait('y', 'dbgg')
            return True
        r.wait('v', f'{tag}g15')
        with LP():
            inst = v.tensor_copy(out=stg[0][:, 0:1024], in_=gfin) if v else None
        r.mark('vm', f'{tag}g16', inst)
        gb16 = stg[0][:, 0:1024]
        for i in range(NT):
            half = (i // 4) % 2
            col = (i // 8) * 512 + (i % 4) * 128
            which = i % 2
            if i >= 2:
                r.wait('t', f'{tag}xadd{i - 2}')
            r.wait('t', f'{tag}g16')
            r.wait('t', 'hbe16')
            r.wait('t', 'consts')
            r.wait('t', wout_ev)
            for nch in range(2):
                inst = t.matmul(pzh[which][:, nch * 512:(nch + 1) * 512],
                                gb16[64 * half:64 * half + 64, col:col + 128],
                                woutb[64 * half:64 * half + 64, nch * 512:(nch + 1) * 512],
                                start=True, stop=False) if t else None
                inst = t.matmul(pzh[which][:, nch * 512:(nch + 1) * 512],
                                ones16[0:1, 0:128], hbe16[0:1, nch * 512:(nch + 1) * 512],
                                start=False, stop=True, tile_position=(0, 0)) if t else None
            r.mark('tm', f'{tag}out{i}', inst)
            r.wait('v', f'{tag}out{i}')
            inst = v.tensor_add(xt(i), xt(i), pzh[which][:, :]) if v else None
            r.mark('vm', f'{tag}xadd{i}', inst)

    if DBG == 'WIN':
        for k in range(8):
            r.wait('g', 'woutd')
            inst = g.dma_start(out=out_d[k * 128:(k + 1) * 128, 0:64], in_=Wint[k]) if g else None
            r.mark('lst', f'dbgw{k}', inst, 16)
        r.wait('g', 'dbgw7')
        return

    layernorm('L1', lambda i: 'x7' if i < 8 else 'x15')
    if DBG == 'LN1':
        for i in range(NT):
            r.wait('g', f'L1ln{i}')
            inst = g.dma_start(out=out_d[i * 128:(i + 1) * 128, :], in_=LNx[i]) if g else None
            r.mark('lst', f'dbgln{i}', inst, 16)
        r.wait('g', f'dbgln{NT - 1}')
        return
    transpose_lnx('L1')
    if DBG == 'LNT':
        for j in range(8):
            r.wait('g', f'L1tpe{j * 2}')
            r.wait('g', f'L1tpe{j * 2 + 1}')
            inst = g.dma_start(out=out_d[j * 256:(j + 1) * 256, :].rearrange("(p b) c -> p (b c)", b=2),
                               in_=LNxT[j]) if g else None
            r.mark('lst', f'dbglnt{j}', inst, 16)
        r.wait('g', 'dbglnt7')
        return
    if ffn('L1', Wint, Woutb0, 'woutd', 'woutd'):
        return

    def dbg_dump_x(pref):
        for i in range(NT):
            r.wait('y', f'{pref}{i}')
            inst = y.dma_start(out=out_d[i * 128:(i + 1) * 128, :], in_=xt(i)) if y else None
            r.mark('lst', f'dbgst{i}', inst, 16)
        r.wait('y', f'dbgst{NT - 1}')

    if DBG == 'P1':
        dbg_dump_x('L1xadd')
        return

    def slice_load(h):
        if h == 0:
            r.wait('y', 'P2sp10')
            r.wait('y', 'P2sp11')
        sem = 'lslA' if h % 2 == 0 else 'lslB'
        inst = y.dma_start(out=QTs[h % 2][0:64, :], in_=qt_d[64 * h:64 * h + 64, :]) if y else None
        r.mark(sem, f'sl{h}q', inst, 16)
        inst = y.dma_start(out=KTs[h % 2][0:64, :], in_=kt_d[64 * h:64 * h + 64, :]) if y else None
        r.mark(sem, f'sl{h}k', inst, 16)

    layernorm('L2', lambda i: f'L1xadd{i}')
    transpose_lnx('L2')
    for pi, (wt, bcv, dst_d, bev, wev) in enumerate((
            (Wqt, bqc, qt_d, 'bq', 'wq7'), (Wkt, bkc, kt_d, 'bk', 'wk7'),
            (Wvt, bvc, vt_d, 'bv', 'wv7'))):
        for m in range(4):
            idx = pi * 4 + m
            which = idx % 2
            psrc = pzBig if which == 0 else pzB2
            if idx >= 2:
                r.wait('t', f'P2ev{idx - 2}')
            if idx == 0:
                r.wait('t', 'L1xadd15')
            r.wait('t', 'woutd')
            for c in range(4):
                for k in range(8):
                    r.wait('t', f'L2tpe{k * 2 + (c // 2)}')
                    inst = t.matmul(psrc[:, c * 512:(c + 1) * 512],
                                    wt[k][:, m * 128:(m + 1) * 128],
                                    LNxT[k][:, c * 512:(c + 1) * 512],
                                    start=(k == 0), stop=(k == 7)) if t else None
            r.mark('tm', f'P2mm{idx}', inst)
            r.wait('s', f'P2mm{idx}')
            r.wait('s', 'bout')
            if idx >= 2:
                r.wait('s', f'P2sp{idx - 2}')
            with LP():
                inst = sc.activation(out=stg[which][:, :], in_=psrc[:, :], func=AF.Identity,
                                     bias=bcv[:, m:m + 1]) if sc else None
            r.mark('sm', f'P2ev{idx}', inst)
            r.wait('y', f'P2ev{idx}')
            inst = y.dma_start(out=dst_d[m * 128:(m + 1) * 128, :], in_=stg[which][:, :]) if y else None
            r.mark('lspA' if which == 0 else 'lspB', f'P2sp{idx}', inst, 16)
    av_tail = [None]
    slice_load(0)

    for i in range(NT):
        r.wait('y', f'L2ln{i}')
        inst = y.dma_start(out=xs_d[i * 128:(i + 1) * 128, :], in_=xt(i)) if y else None
        r.mark('lxs', f'xsp{i}', inst, 16)

    if DBG == 'QT':
        r.wait('g', 'P2sp10')
        r.wait('g', 'P2sp11')
        inst = g.dma_start(out=out_d[0:1024, :],
                           in_=qt_d.rearrange("a (b c) -> (a b) c", c=1024)) if g else None
        r.mark('lst', 'dbgqt', inst, 16)
        inst = g.dma_start(out=out_d[1024:2048, :],
                           in_=kt_d.rearrange("a (b c) -> (a b) c", c=1024)) if g else None
        r.mark('lst', 'dbgkt', inst, 16)
        r.wait('g', 'dbgkt')
        return



    for h in range(H):
        QTh = QTs[h % 2][0:64, :]
        KTh = KTs[h % 2][0:64, :]
        if h > 0:
            r.wait('y', f'h{h - 1}vtp')
        inst = y.dma_start(out=VTs[0:64, :], in_=vt_d[64 * h:64 * h + 64, :]) if y else None
        r.mark('lvt', f'vts{h}', inst, 16)

        def logits_tile(i):
            idx = i
            which = idx % 2
            psrc = pzBig if which == 0 else pzB2
            if idx == 0:
                r.wait('t', f'sl{h}k')
                if h == 0:
                    r.wait('t', 'P2ev10')
                    r.wait('t', 'P2ev11')
                else:
                    r.wait('t', f'h{h - 1}ubce3')
            if idx == 1 and h > 0:
                r.wait('t', f'h{h - 1}aslc')
            if idx >= 2:
                r.wait('t', f'h{h}ex{idx - 2}')
            for c in range(4):
                inst = t.matmul(psrc[:, c * 512:(c + 1) * 512],
                                QTh[:, i * 128:(i + 1) * 128],
                                KTh[:, c * 512:(c + 1) * 512],
                                start=True, stop=True) if t else None
            r.mark('tm', f'h{h}lg{idx}', inst)
            r.wait('s', f'h{h}lg{idx}')
            if h == 0 and idx == 8:
                r.wait('s', 'xsp15')
            if idx == 0 and h > 0:
                r.wait('s', f'h{h - 1}p{NS - 1}')
            with LP():
                inst = sc.activation(out=Atile[i], in_=psrc[:, :], func=AF.Exp,
                                     scale=0.125) if sc else None
            r.mark('sm', f'h{h}ex{idx}', inst)

        logits_tile(0)
        if av_tail[0] is not None:
            av_tail[0]()
            av_tail[0] = None
        for i in range(1, NT):
            logits_tile(i)
        if h + 1 < H:
            if h >= 1:
                r.wait('y', f'h{h - 1}lg{NT - 1}')
            slice_load(h + 1)
        r.wait('t', f'h{h}ex{NT - 1}')
        r.wait('t', f'h{h}ex{NT - 2}')
        for ti in range(NT):
            for half in range(2):
                if ti > 0:
                    r.wait('t', f'h{h}bev{ti - 1}_{half}')
                elif h > 0:
                    r.wait('t', f'h{h - 1}gve{half}' if False else f'h{h - 1}aslc')
                for j in range(8 * half, 8 * half + 8):
                    inst = t.transpose(pb[half][:, (j - 8 * half) * 128:(j - 8 * half + 1) * 128],
                                       Atile[j][:, ti * 128:(ti + 1) * 128], ident[:, :]) if t else None
                r.mark('tm', f'h{h}btp{ti}_{half}', inst)
                r.wait('v', f'h{h}btp{ti}_{half}')
                with LP():
                    inst = v.tensor_copy(out=Btile[ti][:, half * 1024:(half + 1) * 1024],
                                         in_=pb[half][:, :]) if v else None
                r.mark('vm', f'h{h}bev{ti}_{half}', inst)

        inst = v.memset(gcol16[:, :], 1.0) if v else None
        r.mark('vm', f'h{h}rinit', inst)
        for s in range(NS):
            mat = Btile if s % 2 == 0 else Atile
            vecap = gcol16 if s % 2 == 0 else ucol16
            r.wait('t', f'h{h}r{s - 1}' if s > 0 else f'h{h}rinit')
            if s > 0:
                r.wait('t', f'h{h}evs{s - 1}_1')
                r.wait('t', f'h{h}evv{s - 1}_1')
            for ti in range(NT):
                if s == 0:
                    r.wait('t', f'h{h}bev{ti}_0')
                    r.wait('t', f'h{h}bev{ti}_1')
                for c in range(4):
                    inst = t.matmul(pz[c][32 * c:32 * c + 1, :], vecap[:, ti:ti + 1],
                                    mat[ti][:, c * 512:(c + 1) * 512],
                                    start=(ti == 0), stop=(ti == NT - 1),
                                    tile_position=(0, 32 * c)) if t else None
            r.mark('tm', f'h{h}p{s}', inst)
            for w in range(NWARM):
                if t:
                    t.matmul(pwarm[0:1, :], gcol16[:, 0:1],
                             Btile[0][:, (w % 4) * 512:(w % 4 + 1) * 512],
                             start=True, stop=True, tile_position=(0, 0))
            if True:
                for gq in range(4):
                    if gq < 2:
                        r.wait('t', f'h{h}evs{s}_{gq}')
                    else:
                        r.wait('t', f'h{h}evv{s}_{gq - 2}')
                    if gq == 0 and s > 0:
                        r.wait('t', f'h{h}r{s - 1}')
                    p = 32 * gq
                    for j in range(4 * gq, 4 * gq + 4):
                        inst = t.matmul(pt[:, j:j + 1],
                                        yrow[p:p + 1, j * 128:(j + 1) * 128],
                                        ones16[p:p + 1, 0:1],
                                        start=True, stop=True, tile_position=(p, 0)) if t else None
                    r.mark('tm', f'h{h}tp{s}_{gq}', inst)
            r.wait('s', f'h{h}p{s}')
            for c in range(2):
                with LP():
                    inst = sc.copy(out=yrow[32 * c:32 * c + 1, c * 512:(c + 1) * 512],
                                   in_=pz[c][32 * c:32 * c + 1, :]) if sc else None
                r.mark('sm', f'h{h}evs{s}_{c}', inst)
            r.wait('v', f'h{h}p{s}')
            for c in range(2, 4):
                with LP():
                    inst = v.tensor_copy(out=yrow[32 * c:32 * c + 1, c * 512:(c + 1) * 512],
                                         in_=pz[c][32 * c:32 * c + 1, :]) if v else None
                r.mark('vm', f'h{h}evv{s}_{c - 2}', inst)
            for gq in range(4):
                r.wait('v', f'h{h}tp{s}_{gq}')
            if s == NS - 1:
                inst = v.reciprocal(out=gcol32[:, :], in_=pt[:, :]) if v else None
            else:
                col16 = ucol16 if s % 2 == 0 else gcol16
                with LP():
                    inst = v.reciprocal(out=col16[:, :], in_=pt[:, :]) if v else None
            r.mark('vm', f'h{h}r{s}', inst)
            if s == 2:
                r.wait('t', f'vts{h}')
                if h > 0:
                    r.wait('t', f'h{h - 1}vke')
                for j in range(16):
                    inst = t.transpose(pb[0][:, j * 64:(j + 1) * 64],
                                       VTs[0:64, j * 128:(j + 1) * 128], ident[0:64, 0:64]) if t else None
                r.mark('tm', f'h{h}vtp', inst)
                r.wait('v', f'h{h}vtp')
                with LP():
                    inst = v.tensor_copy(out=Vk[:, :], in_=pb[0][:, 0:1024]) if v else None
                r.mark('vm', f'h{h}vke', inst)
            if s == NS - 1:
                r.wait('t', f'h{h}vke')
                for j in range(16):
                    dst = pb[0] if j < 8 else pb[1]
                    inst = t.transpose(dst[0:1, (j % 8) * 128:(j % 8 + 1) * 128],
                                       ucol16[:, j:j + 1], ident[:, :]) if t else None
                r.mark('tm', f'h{h}utp', inst)
                r.wait('s', f'h{h}utp')
                for half in range(2):
                    with LP():
                        inst = sc.copy(out=urow16[0:1, half * 1024:(half + 1) * 1024],
                                       in_=(pb[0] if half == 0 else pb[1])[0:1, :]) if sc else None
                r.mark('sm', f'h{h}uev', inst)
                r.wait('t', f'h{h}uev')
                r.wait('t', f'h{h}r{NS - 1}')
                r.wait('t', f'h{h}evs{NS - 1}_1')
                r.wait('t', f'h{h}evv{NS - 1}_1')
                for c in range(4):
                    inst = t.matmul(pz[c][0:64, :], ones16[0:1, 0:64],
                                    urow16[0:1, c * 512:(c + 1) * 512],
                                    start=True, stop=True, tile_position=(0, 0)) if t else None
                    r.mark('tm', f'h{h}ubc{c}', inst)
                    r.wait('v', f'h{h}ubc{c}')
                    with LP():
                        inst = v.tensor_copy(out=ubc16[0:64, c * 512:(c + 1) * 512],
                                             in_=pz[c][0:64, :]) if v else None
                    r.mark('vm', f'h{h}ubce{c}', inst)

        def make_av_tail(hh):
            def emit():
                h = hh
                r.wait('v', f'h{h}r{NS - 1}')
                r.wait('v', f'h{h}vke')
                inst = None
                for j in range(16):
                    with LP():
                        inst = v.tensor_scalar_mul(gV[:, j * 64:(j + 1) * 64],
                                                   Vk[:, j * 64:(j + 1) * 64],
                                                   gcol32[:, j:j + 1]) if v else None
                r.mark('vm', f'h{h}gvs', inst)
                r.wait('t', f'h{h}gvs')
                r.wait('t', f'h{h}uev')
                for c in range(4):
                    for ti in range(NT):
                        inst = t.matmul(pzc[c][0:64, :], gV[:, ti * 64:(ti + 1) * 64],
                                        Btile[ti][:, c * 512:(c + 1) * 512],
                                        start=(ti == 0), stop=(ti == NT - 1)) if t else None
                r.mark('tm', f'h{h}avmm', inst)
                r.wait('v', f'h{h}avmm')
                for c in range(4):
                    r.wait('v', f'h{h}ubce{c}')
                if h > 0:
                    r.wait('v', f'h{h - 1}asp')
                with LP():
                    inst = v.tensor_mul(aslc[0:64, :], pzB2[0:64, :], ubc16[0:64, :]) if v else None
                r.mark('vm', f'h{h}aslc', inst)
                r.wait('y', f'h{h}aslc')
                inst = y.dma_start(out=at_d[64 * h:64 * h + 64, :], in_=aslc[0:64, :]) if y else None
                r.mark('lat', f'h{h}asp', inst, 16)
            return emit
        av_tail[0] = make_av_tail(h)

    if av_tail[0] is not None:
        av_tail[0]()
        av_tail[0] = None

    if DBG == 'H0':
        r.wait('g', 'h0asp')
        inst = g.dma_start(out=out_d[0:128, :],
                           in_=at_d[0:64, :].rearrange("a (b c) -> (a b) c", c=1024)) if g else None
        r.mark('lst', 'dbgh0', inst, 16)
        r.wait('g', 'dbgh0')
        return

    r.wait('y', f'h{H - 1}asp')
    for j in range(4):
        inst = y.dma_start(out=atT[j], in_=at_d[j * 128:(j + 1) * 128, :]) if y else None
        r.mark('lat', f'atl{j}', inst, 16)
    for i in range(NT):
        inst = y.dma_start(out=xt(i), in_=xs_d[i * 128:(i + 1) * 128, :]) if y else None
        r.mark('ldx', f'xrl{i}', inst, 16)
    r.wait('g', f'h{H - 1}aslc')
    for j in range(4):
        inst = g.dma_start(out=Wot[j], in_=wo_d[j * 128:(j + 1) * 128, :]) if g else None
        r.mark('ldw', f'wo{j}', inst, 16)
    inst = g.dma_start(out=Woutb2[0:64, :], in_=wout_d[:, :]) if g else None
    r.mark('ldw', 'wout2', inst, 16)
    inst = g.dma_start(out=Woutb2[64:128, :], in_=wout_d[:, :]) if g else None
    r.mark('ldw', 'wout2d', inst, 16)
    for k in range(8):
        inst = g.dma_start(out=Winb2[k], in_=win_d[k * 128:(k + 1) * 128, :]) if g else None
        r.mark('ldw', f'win2_{k}', inst, 16)

    for i in range(NT):
        which = i % 2
        if i >= 2:
            r.wait('t', f'P4xa{i - 2}')
        if i == 0:
            r.wait('t', 'atl3')
            r.wait('t', 'win2_7')
            r.wait('t', 'bo16')
            r.wait('t', f'h{H - 1}aslc')
        for nch in range(2):
            for j in range(4):
                inst = t.matmul(pzh[which][:, nch * 512:(nch + 1) * 512],
                                atT[j][:, i * 128:(i + 1) * 128],
                                Wot[j][:, nch * 512:(nch + 1) * 512],
                                start=(j == 0), stop=False) if t else None
            inst = t.matmul(pzh[which][:, nch * 512:(nch + 1) * 512],
                            ones16[0:1, 0:128], bo16[0:1, nch * 512:(nch + 1) * 512],
                            start=False, stop=True, tile_position=(0, 0)) if t else None
        r.mark('tm', f'P4mm{i}', inst)
        r.wait('v', f'P4mm{i}')
        r.wait('v', 'xrl15')
        inst = v.tensor_add(xt(i), xt(i), pzh[which][:, :]) if v else None
        r.mark('vm', f'P4xa{i}', inst)

    if DBG == 'P4':
        dbg_dump_x('P4xa')
        return

    layernorm('L5', lambda i: f'P4xa{i}', guard='P4mm15')
    transpose_lnx('L5')
    ffn('L5', Winb2, Woutb2, 'win2_7', 'win2_7', pzfree='P4xa15')

    for i in range(NT):
        r.wait('y', f'L5xadd{i}')
        inst = y.dma_start(out=out_d[i * 128:(i + 1) * 128, :], in_=xt(i)) if y else None
        r.mark('lst', f'st{i}', inst, 16)
    r.wait('y', f'st{NT - 1}')


def build_nc():
    nc = bass.Bass("TRN2", target_bir_lowering=False, debug=False)
    names = [("x_in", [T, D], F32, "ExternalInput"),
             ("wq", [D, 512], F32, "ExternalInput"),
             ("bq", [4, 128], F32, "ExternalInput"),
             ("wk", [D, 512], F32, "ExternalInput"),
             ("bk", [4, 128], F32, "ExternalInput"),
             ("wv", [D, 512], F32, "ExternalInput"),
             ("bv", [4, 128], F32, "ExternalInput"),
             ("wo", [512, D], F32, "ExternalInput"),
             ("bo", [D], F32, "ExternalInput"),
             ("win", [D, 64], F32, "ExternalInput"),
             ("binp", [128, 1], F32, "ExternalInput"),
             ("w1", [128, 16], F32, "ExternalInput"),
             ("b1", [128, 16], F32, "ExternalInput"),
             ("w2", [128, 16], F32, "ExternalInput"),
             ("b2", [64, 1], F32, "ExternalInput"),
             ("wout", [64, D], F32, "ExternalInput"),
             ("bout", [D], F32, "ExternalInput"),
             ("identb", [128, 128], BF16, "ExternalInput"),
             ("out", [T, D], F32, "ExternalOutput"),
             ("xs", [T, D], F32, "Internal"),
             ("qts", [512, T], BF16, "Internal"),
             ("kts", [512, T], BF16, "Internal"),
             ("vts", [512, T], BF16, "Internal"),
             ("ats", [512, T], BF16, "Internal")]
    io = [nc.dram_tensor(n, s, d, kind=k).ap() for n, s, d, k in names]

    with contextlib.ExitStack() as stack:
        e = stack.enter_context
        A = e(nc.sbuf_tensor("Areg", [128, 32768], BF16))
        Bm = e(nc.sbuf_tensor("Breg", [128, 32768], BF16))
        Q = e(nc.sbuf_tensor("Qreg", [128, 30208], BF16))
        M = e(nc.sbuf_tensor("Mreg", [128, 9216], BF16))
        pzBig = e(nc.psum_tensor("pzBig", [128, 2048], F32))
        pzB2 = e(nc.psum_tensor("pzB2", [128, 2048], F32))
        sems = {nm: e(nc.semaphore(nm)) for nm in
                ('ldx', 'ldx2', 'lxs', 'ldsm', 'ldw', 'lspA', 'lspB', 'lslA', 'lslB',
                 'lvt', 'lat', 'lst', 'tm', 'sm', 'vm')}
        block = e(nc.Block())
        bufs = (A[:, :], Bm[:, :], Q[:, :], M[:, :], pzBig[:, :], pzB2[:, :])

        planner = Run()
        body(planner, nc, io, bufs)
        plan = planner.plan

        @block.tensor
        def _(E):
            body(Run('t', plan, sems, E), nc, io, bufs)

        @block.scalar
        def _(E):
            body(Run('s', plan, sems, E), nc, io, bufs)

        @block.vector
        def _(E):
            body(Run('v', plan, sems, E), nc, io, bufs)

        @block.sync
        def _(E):
            body(Run('y', plan, sems, E), nc, io, bufs)

        @block.gpsimd
        def _(E):
            body(Run('g', plan, sems, E), nc, io, bufs)

    return nc


def kernel(L0_attn_x, L0_attn_Wq, L0_attn_bq, L0_attn_Wk, L0_attn_bk,
           L0_attn_Wv, L0_attn_bv, L0_attn_Wo, L0_attn_bo,
           L0_ffn_Win, L0_ffn_bin_, L0_ffn_W1, L0_ffn_b1, L0_ffn_W2,
           L0_ffn_b2, L0_ffn_Wout, L0_ffn_bout):
    x = np.ascontiguousarray(np.asarray(L0_attn_x, dtype=np.float32))
    common = {
        "wq": np.ascontiguousarray(np.asarray(L0_attn_Wq, np.float32)),
        "bq": np.ascontiguousarray(np.asarray(L0_attn_bq, np.float32).reshape(4, 128)),
        "wk": np.ascontiguousarray(np.asarray(L0_attn_Wk, np.float32)),
        "bk": np.ascontiguousarray(np.asarray(L0_attn_bk, np.float32).reshape(4, 128)),
        "wv": np.ascontiguousarray(np.asarray(L0_attn_Wv, np.float32)),
        "bv": np.ascontiguousarray(np.asarray(L0_attn_bv, np.float32).reshape(4, 128)),
        "wo": np.ascontiguousarray(np.asarray(L0_attn_Wo, np.float32)),
        "bo": np.ascontiguousarray(np.asarray(L0_attn_bo, np.float32)),
        "win": np.ascontiguousarray(np.asarray(L0_ffn_Win, np.float32)),
        "binp": np.ascontiguousarray(np.tile(np.asarray(L0_ffn_bin_, np.float32).reshape(64, 1), (2, 1))),
        "w1": np.ascontiguousarray(np.tile(np.asarray(L0_ffn_W1, np.float32), (2, 1))),
        "b1": np.ascontiguousarray(np.tile(np.asarray(L0_ffn_b1, np.float32), (2, 1))),
        "w2": np.ascontiguousarray(np.tile(np.asarray(L0_ffn_W2, np.float32), (2, 1))),
        "b2": np.ascontiguousarray(np.asarray(L0_ffn_b2, np.float32).reshape(64, 1)),
        "wout": np.ascontiguousarray(np.asarray(L0_ffn_Wout, np.float32)),
        "bout": np.ascontiguousarray(np.asarray(L0_ffn_bout, np.float32)),
        "identb": np.eye(128, dtype=ml_dtypes.bfloat16),
    }
    nc = build_nc()
    in_maps = [dict(common, x_in=np.ascontiguousarray(x[b])) for b in range(8)]
    res = run_bass_kernel_spmd(nc, in_maps, core_ids=list(range(8)))
    out = np.stack([np.asarray(res.results[b]["out"]) for b in range(8)], axis=0)
    return out.astype(np.float32)


# version 21
# speedup vs baseline: 1.2352x; 1.0250x over previous
import contextlib
import numpy as np
import ml_dtypes
import concourse.bass as bass
import concourse.mybir as mybir
from concourse.bass_utils import run_bass_kernel_spmd

F32 = mybir.dt.float32
BF16 = mybir.dt.bfloat16
AF = mybir.ActivationFunctionType
OP = mybir.AluOpType
AX = mybir.AxisListType.X

DBG = None
T = 2048
D = 1024
H = 8
NT = 16
ITERS = 20
NS = 2 * ITERS
NWARM = 6


class Run:
    def __init__(self, eng=None, plan=None, sems=None, E=None):
        self.eng = eng
        self.planning = plan is None
        self.plan = {} if plan is None else plan
        self.counts = {}
        self.sems = sems
        self.E = E

    def mark(self, sem, event, inst, n=1):
        if self.planning:
            c = self.counts.get(sem, 0) + n
            self.counts[sem] = c
            assert event not in self.plan, f"dup event {event}"
            self.plan[event] = (sem, c)
        if inst is not None:
            inst.then_inc(self.sems[sem], n)

    def wait(self, eng, event):
        if self.planning or eng != self.eng:
            return
        sem, c = self.plan[event]
        self.E.wait_ge(self.sems[sem], c)

    def own(self, eng):
        return (not self.planning) and eng == self.eng


def body(r, nc, io, bufs):
    t = r.E if r.own('t') else None
    sc = r.E if r.own('s') else None
    v = r.E if r.own('v') else None
    y = r.E if r.own('y') else None
    g = r.E if r.own('g') else None

    A, Bm, Q, M, pzBig, pzB2 = bufs

    Af32 = A.bitcast(F32)
    Bf32 = Bm.bitcast(F32)
    LNx = [A[:, i * 1024:(i + 1) * 1024] for i in range(NT)]
    LNxT = [Bm[:, j * T:(j + 1) * T] for j in range(8)]
    Atile = [A[:, i * T:(i + 1) * T] for i in range(NT)]
    Btile = [Bm[:, i * T:(i + 1) * T] for i in range(NT)]
    atT = [A[:, j * T:(j + 1) * T] for j in range(4)]

    def xt(i):
        if i < 8:
            return Af32[:, 8192 + i * 1024: 8192 + (i + 1) * 1024]
        return Bf32[:, 8192 + (i - 8) * 1024: 8192 + (i - 7) * 1024]

    s2 = Af32[:, 0:1024]
    hb = [Af32[:, 1024:2048], Af32[:, 2048:3072]]
    gA = Af32[:, 3072:4096]
    gB = Af32[:, 4096:5120]
    Qf32 = Q.bitcast(F32)
    bout_r = Qf32[0:1, 4608:5632]
    hbe_r = Qf32[0:1, 5632:6656]
    tmp_r = Qf32[0:1, 6656:7680]
    bo_r = Qf32[0:1, 7680:8704]
    Woutb0 = Q[:, 8192:9216]

    stg = [Q[:, 0:2048], Q[:, 2048:4096]]
    QTs = [Q[:, 0:2048], Q[:, 2048:4096]]
    KTs = [Q[:, 4096:6144], Q[:, 6144:8192]]
    VTs = Q[:, 8192:10240]
    Vk = Q[:, 10240:11264]
    gV = Q[:, 14336:15360]
    aslc = Q[:, 15360:17408]
    ubc16 = Q[:, 12288:14336]
    urow16 = Q[:, 21504:23552]
    Wqt = [Q[:, 17408 + k * 512:17408 + (k + 1) * 512] for k in range(8)]
    Wkt = [Q[:, 21504 + k * 512:21504 + (k + 1) * 512] for k in range(8)]
    Wvt = [Q[:, 25600 + k * 512:25600 + (k + 1) * 512] for k in range(8)]
    Wint = [Q[:, 29696 + k * 64:29696 + (k + 1) * 64] for k in range(8)]
    ubc32 = Q[:, 17408:21504].bitcast(F32)
    Wot = [Q[:, j * 1024:(j + 1) * 1024] for j in range(4)]
    Woutb2 = Q[:, 4096:5120]
    Winb2 = [Q[:, 5120 + k * 64:5120 + (k + 1) * 64] for k in range(8)]

    yrow = M[:, 0:2048]
    ident = M[:, 2048:2176]
    ones16 = M[:, 2176:2304]
    ucol16 = M[:, 2304:2320]
    gcol16 = M[:, 2320:2336]
    bo16 = M[:, 2336:3360]
    hbe16 = M[:, 3360:4384]
    b2c16 = M[:, 4384:4385]
    Mf = M[:, 4608:9216].bitcast(F32)
    ones32 = Mf[:, 0:128]
    gcol32 = Mf[:, 128:144]
    bnst = Mf[:, 144:156]
    bnagg = Mf[:, 160:192]
    mean = Mf[:, 1152:1168]
    var = Mf[:, 1168:1184]
    rstd = Mf[:, 1184:1200]
    nms = Mf[:, 1200:1216]
    msq = Mf[:, 1216:1232]
    bqc = Mf[:, 1232:1236]
    bkc = Mf[:, 1236:1240]
    bvc = Mf[:, 1240:1244]
    binc = Mf[:, 1244:1245]
    b2c = Mf[:, 1245:1246]
    w1c = Mf[:, 1248:1264]
    b1c = Mf[:, 1264:1280]
    w2h = Mf[:, 1280:1296]
    w2r = Mf[:, 1296:1312]

    pz = [pzBig[:, c * 512:(c + 1) * 512] for c in range(4)]
    pzh = [pzBig[:, 0:1024], pzBig[:, 1024:2048]]
    pt = pzB2[:, 0:16]
    pwarm = pzB2[:, 512:1024]
    pb = [pzB2[:, 1024:1536].bitcast(BF16), pzB2[:, 1536:2048].bitcast(BF16)]
    pzc = [pzB2[:, c * 512:(c + 1) * 512] for c in range(4)]

    (x_d, wq_d, bq_d, wk_d, bk_d, wv_d, bv_d, wo_d, bo_d,
     win_d, bin_d, w1_d, b1_d, w2_d, b2_d, wout_d, bout_d,
     ident_d, out_d, xs_d, qt_d, kt_d, vt_d, at_d) = io

    LP = lambda: nc.allow_low_precision(reason="bf16 matmul-operand staging (blessed)")

    for i in range(NT):
        inst = y.dma_start(out=xt(i), in_=x_d[i * 128:(i + 1) * 128, :]) if y else None
        r.mark('ldx' if i < 8 else 'ldx2', f'x{i}', inst, 16)
    inst = y.dma_start(out=ident[:, :], in_=ident_d[:, :]) if y else None
    r.mark('ldsm', 'ident', inst, 16)
    inst = y.dma_start(out=binc[:, :], in_=bin_d[:, :]) if y else None
    r.mark('ldsm', 'bin0', inst, 16)
    inst = y.dma_start(out=b2c[0:64, :], in_=b2_d[0:64, :]) if y else None
    r.mark('ldsm', 'b2', inst, 16)
    for nm, dst, src in (('w1', w1c, w1_d), ('b1', b1c, b1_d), ('w2', w2r, w2_d)):
        inst = y.dma_start(out=dst[:, :], in_=src[:, :]) if y else None
        r.mark('ldsm', f'{nm}0', inst, 16)
    inst = y.dma_start(out=bo_r, in_=bo_d.rearrange("(a d) -> a d", a=1)) if y else None
    r.mark('ldsm', 'bo', inst, 16)
    inst = y.dma_start(out=bout_r, in_=bout_d.rearrange("(a d) -> a d", a=1)) if y else None
    r.mark('ldsm', 'bout', inst, 16)
    for nm, dst, src in (('bq', bqc, bq_d), ('bk', bkc, bk_d), ('bv', bvc, bv_d)):
        inst = y.dma_start(out=dst[:, :], in_=src.rearrange("m p -> p m"),
                           allow_slow_non_contiguous=True) if y else None
        r.mark('ldb', nm, inst, 16)
    inst = g.dma_start(out=Woutb0[0:64, :], in_=wout_d[:, :]) if g else None
    r.mark('ldw0', 'wout', inst, 16)
    inst = g.dma_start(out=Woutb0[64:128, :], in_=wout_d[:, :]) if g else None
    r.mark('ldw0', 'woutd', inst, 16)
    for k in range(8):
        inst = g.dma_start(out=Wqt[k], in_=wq_d[k * 128:(k + 1) * 128, :]) if g else None
        r.mark('ldw', f'wq{k}', inst, 16)
        inst = g.dma_start(out=Wkt[k], in_=wk_d[k * 128:(k + 1) * 128, :]) if g else None
        r.mark('ldw', f'wk{k}', inst, 16)
        inst = g.dma_start(out=Wvt[k], in_=wv_d[k * 128:(k + 1) * 128, :]) if g else None
        r.mark('ldw', f'wv{k}', inst, 16)
        inst = g.dma_start(out=Wint[k], in_=win_d[k * 128:(k + 1) * 128, :]) if g else None
        r.mark('ldw', f'win{k}', inst, 16)

    if v:
        v.memset(ones16[:, :], 1.0)
        v.memset(ones32[:, :], 1.0)
    inst = v.memset(yrow[:, :], 0.0) if v else None
    r.mark('vm', 'consts', inst)

    def prep_tail():
        r.wait('v', 'bout')
        inst = v.tensor_scalar_mul(w2h[:, :], w2r[:, :], 0.5) if v else None
        r.mark('vm', 'w2half', inst)
        r.wait('v', 'bout')
        with LP():
            inst = v.tensor_copy(out=b2c16[0:64, :], in_=b2c[0:64, :]) if v else None
        r.mark('vm', 'b2c16', inst)
        r.wait('t', 'b2c16')
        r.wait('t', 'woutd')
        for c in range(2):
            inst = t.matmul(pzc[c][0:1, :], b2c16[0:64, 0:1], Woutb0[0:64, c * 512:(c + 1) * 512],
                            start=True, stop=True) if t else None
            r.mark('tm', f'wtb2_{c}', inst)
        r.wait('s', 'bout')
        inst = sc.mul(tmp_r, bout_r, 0.5) if sc else None
        r.mark('sm', 'halfbout', inst)
        r.wait('v', 'wtb2_0')
        r.wait('v', 'wtb2_1')
        r.wait('v', 'halfbout')
        for c in range(2):
            with LP():
                inst = v.scalar_tensor_tensor(out=hbe_r[:, c * 512:(c + 1) * 512],
                                              in0=pzc[c][0:1, :], scalar=0.5,
                                              in1=tmp_r[:, c * 512:(c + 1) * 512],
                                              op0=OP.mult, op1=OP.add) if v else None
            r.mark('vm', f'hbe{c}', inst)
        r.wait('s', 'hbe0')
        r.wait('s', 'hbe1')
        with LP():
            inst = sc.copy(out=hbe16[0:1, :], in_=hbe_r) if sc else None
        r.mark('sm', 'hbe16', inst)
        r.wait('s', 'bout')
        with LP():
            inst = sc.copy(out=bo16[0:1, :], in_=bo_r) if sc else None
        r.mark('sm', 'bo16', inst)

    def warm(n):
        if t:
            for w in range(n):
                t.matmul(pwarm[0:1, :], ones16[:, 0:1], yrow[:, (w % 3) * 512:(w % 3) * 512 + 512],
                         start=True, stop=True, tile_position=(0, 0))

    def ln_stats_tile(tag, i, xev_ev):
        r.wait('v', xev_ev)
        if i > 0:
            r.wait('v', f'{tag}ag{i - 1}')
        inst = v.bn_stats(bnst[:, 0:6], xt(i)[:, 0:512]) if v else None
        r.mark('vm', f'{tag}ba{i}', inst)
        r.wait('v', f'{tag}ba{i}')
        inst = v.bn_stats(bnst[:, 6:12], xt(i)[:, 512:1024]) if v else None
        r.mark('vm', f'{tag}bb{i}', inst)
        r.wait('v', f'{tag}bb{i}')
        inst = v.bn_aggr(bnagg[:, 2 * i:2 * i + 2], bnst[:, 0:12]) if v else None
        r.mark('vm', f'{tag}ag{i}', inst)

    def ln_finish(tag):
        meanv = bnagg[:, 0:32:2]
        varv = bnagg[:, 1:32:2]
        r.wait('v', f'{tag}ag{NT - 1}')
        inst = v.tensor_scalar_add(var[:, :], varv, 1e-5) if v else None
        r.mark('vm', f'{tag}m5', inst)
        r.wait('s', f'{tag}m5')
        inst = sc.sqrt(msq[:, :], var[:, :]) if sc else None
        r.mark('sm', f'{tag}m6', inst)
        r.wait('v', f'{tag}m6')
        inst = v.reciprocal(rstd[:, :], msq[:, :]) if v else None
        r.mark('vm', f'{tag}m7', inst)
        r.wait('v', f'{tag}m7')
        with LP():
            inst = v.scalar_tensor_tensor(out=nms[:, :], in0=meanv, scalar=-1.0,
                                          in1=rstd[:, :], op0=OP.mult, op1=OP.mult) if v else None
        r.mark('vm', f'{tag}m8', inst)
        for i in range(NT):
            r.wait('s', f'{tag}m8')
            with LP():
                inst = sc.activation(out=LNx[i], in_=xt(i), func=AF.Identity,
                                     scale=rstd[:, i:i + 1], bias=nms[:, i:i + 1]) if sc else None
            r.mark('sm', f'{tag}ln{i}', inst)

    def layernorm(tag, xev):
        for i in range(NT):
            ln_stats_tile(tag, i, xev(i))
        ln_finish(tag)

    def transpose_lnx(tag):
        r.wait('t', 'bout')
        for j in range(8):
            for half in range(2):
                idx = j * 2 + half
                which = idx % 2
                if idx >= 2:
                    r.wait('t', f'{tag}tpe{idx - 2}')
                for i in range(8 * half, 8 * half + 8):
                    r.wait('t', f'{tag}ln{i}')
                    inst = t.transpose(pb[which][:, (i - 8 * half) * 128:(i - 8 * half + 1) * 128],
                                       LNx[i][:, j * 128:(j + 1) * 128], ident[:, :]) if t else None
                r.mark('tm', f'{tag}tp{idx}', inst)
                r.wait('v', f'{tag}tp{idx}')
                inst = v.tensor_copy(out=LNxT[j][:, half * 1024:(half + 1) * 1024],
                                     in_=pb[which][:, :]) if v else None
                r.mark('vm', f'{tag}tpe{idx}', inst)

    def ffn(tag, win_tiles, woutb, win_ev, wout_ev, pzfree=None, stats_tag=None):
        for c in range(4):
            r.wait('t', win_ev)
            if c == 0 and pzfree is not None:
                r.wait('t', pzfree)
            for k in range(8):
                r.wait('t', f'{tag}tpe{k * 2 + (c // 2)}')
                inst = t.matmul(pz[c][0:64, :], win_tiles[k][:, :],
                                LNxT[k][:, c * 512:(c + 1) * 512],
                                start=(k == 0), stop=(k == 7)) if t else None
            r.mark('tm', f'{tag}sin{c}', inst)
        for c in range(4):
            r.wait('s', f'{tag}sin{c}')
            r.wait('s', 'bout')
            half, col = c % 2, (c // 2) * 512
            inst = sc.activation(out=s2[64 * half:64 * half + 64, col:col + 512],
                                 in_=pz[c][0:64, :], func=AF.Identity,
                                 bias=binc[64 * half:64 * half + 64, :]) if sc else None
            r.mark('sm', f'{tag}s2_{c}', inst)
        for i in range(16):
            for c in range(4):
                r.wait('s', f'{tag}s2_{c}')
            r.wait('s', 'bout')
            if i >= 2:
                r.wait('s', f'{tag}g{i - 2}')
            inst = sc.activation(out=hb[i % 2], in_=s2[:, :], func=AF.Relu,
                                 scale=w1c[:, i:i + 1], bias=b1c[:, i:i + 1]) if sc else None
            r.mark('sm', f'{tag}h{i}', inst)
            r.wait('v', f'{tag}h{i}')
            gsrc, gdst = (gA, gB) if i % 2 else (gB, gA)
            if i == 0:
                r.wait('v', 'w2half')
                inst = v.tensor_scalar_mul(gdst, hb[i % 2], w2h[:, 0:1]) if v else None
            else:
                r.wait('v', f'{tag}g{i - 1}')
                with LP():
                    inst = v.scalar_tensor_tensor(out=gdst, in0=hb[i % 2], scalar=w2h[:, i:i + 1],
                                                  in1=gsrc, op0=OP.mult, op1=OP.add) if v else None
            r.mark('vm', f'{tag}g{i}', inst)
        gfin = gB
        if DBG == 'S2' and tag == 'L1':
            for c in range(4):
                r.wait('y', f'{tag}s2_{c}')
            inst = y.dma_start(out=out_d[0:128, :], in_=s2) if y else None
            r.mark('lst', 'dbgs2', inst, 16)
            r.wait('v', f'{tag}g15')
            r.wait('y', f'{tag}g15')
            inst = y.dma_start(out=out_d[128:256, :], in_=gfin) if y else None
            r.mark('lst', 'dbgg', inst, 16)
            r.wait('y', 'dbgg')
            return True
        r.wait('v', f'{tag}g15')
        with LP():
            inst = v.tensor_copy(out=stg[0][:, 0:1024], in_=gfin) if v else None
        r.mark('vm', f'{tag}g16', inst)
        gb16 = stg[0][:, 0:1024]
        for i in range(NT):
            half = (i // 4) % 2
            col = (i // 8) * 512 + (i % 4) * 128
            which = i % 2
            if i >= 2:
                r.wait('t', f'{tag}xadd{i - 2}')
            r.wait('t', f'{tag}g16')
            r.wait('t', 'hbe16')
            r.wait('t', 'consts')
            r.wait('t', wout_ev)
            for nch in range(2):
                inst = t.matmul(pzh[which][:, nch * 512:(nch + 1) * 512],
                                gb16[64 * half:64 * half + 64, col:col + 128],
                                woutb[64 * half:64 * half + 64, nch * 512:(nch + 1) * 512],
                                start=True, stop=False) if t else None
                inst = t.matmul(pzh[which][:, nch * 512:(nch + 1) * 512],
                                ones16[0:1, 0:128], hbe16[0:1, nch * 512:(nch + 1) * 512],
                                start=False, stop=True, tile_position=(0, 0)) if t else None
            r.mark('tm', f'{tag}out{i}', inst)
            r.wait('v', f'{tag}out{i}')
            inst = v.tensor_add(xt(i), xt(i), pzh[which][:, :]) if v else None
            r.mark('vm', f'{tag}xadd{i}', inst)
            if stats_tag is not None:
                ln_stats_tile(stats_tag, i, f'{tag}xadd{i}')

    if DBG == 'WIN':
        for k in range(8):
            r.wait('g', 'win7')
            inst = g.dma_start(out=out_d[k * 128:(k + 1) * 128, 0:64], in_=Wint[k]) if g else None
            r.mark('lst', f'dbgw{k}', inst, 16)
        r.wait('g', 'dbgw7')
        return

    layernorm('L1', lambda i: 'x7' if i < 8 else 'x15')
    if DBG == 'LN1':
        for i in range(NT):
            r.wait('g', f'L1ln{i}')
            inst = g.dma_start(out=out_d[i * 128:(i + 1) * 128, :], in_=LNx[i]) if g else None
            r.mark('lst', f'dbgln{i}', inst, 16)
        r.wait('g', f'dbgln{NT - 1}')
        return
    prep_tail()
    transpose_lnx('L1')
    if DBG == 'LNT':
        for j in range(8):
            r.wait('g', f'L1tpe{j * 2}')
            r.wait('g', f'L1tpe{j * 2 + 1}')
            inst = g.dma_start(out=out_d[j * 256:(j + 1) * 256, :].rearrange("(p b) c -> p (b c)", b=2),
                               in_=LNxT[j]) if g else None
            r.mark('lst', f'dbglnt{j}', inst, 16)
        r.wait('g', 'dbglnt7')
        return
    if ffn('L1', Wint, Woutb0, 'win7', 'woutd', stats_tag='L2'):
        return

    def dbg_dump_x(pref):
        for i in range(NT):
            r.wait('y', f'{pref}{i}')
            inst = y.dma_start(out=out_d[i * 128:(i + 1) * 128, :], in_=xt(i)) if y else None
            r.mark('lst', f'dbgst{i}', inst, 16)
        r.wait('y', f'dbgst{NT - 1}')

    if DBG == 'P1':
        dbg_dump_x('L1xadd')
        return

    def slice_load(h):
        if h == 0:
            r.wait('y', 'P2sp10')
            r.wait('y', 'P2sp11')
        sem = 'lslA' if h % 2 == 0 else 'lslB'
        inst = y.dma_start(out=QTs[h % 2][0:64, :], in_=qt_d[64 * h:64 * h + 64, :]) if y else None
        r.mark(sem, f'sl{h}q', inst, 16)
        inst = y.dma_start(out=KTs[h % 2][0:64, :], in_=kt_d[64 * h:64 * h + 64, :]) if y else None
        r.mark(sem, f'sl{h}k', inst, 16)

    ln_finish('L2')
    transpose_lnx('L2')
    for pi, (wt, bcv, dst_d, bev, wev) in enumerate((
            (Wqt, bqc, qt_d, 'bq', 'wq7'), (Wkt, bkc, kt_d, 'bk', 'wk7'),
            (Wvt, bvc, vt_d, 'bv', 'wv7'))):
        for m in range(4):
            idx = pi * 4 + m
            which = idx % 2
            psrc = pzBig if which == 0 else pzB2
            if idx >= 2:
                r.wait('t', f'P2ev{idx - 2}')
            if idx == 0:
                r.wait('t', 'L1xadd15')
            r.wait('t', 'win7')
            for c in range(4):
                for k in range(8):
                    r.wait('t', f'L2tpe{k * 2 + (c // 2)}')
                    inst = t.matmul(psrc[:, c * 512:(c + 1) * 512],
                                    wt[k][:, m * 128:(m + 1) * 128],
                                    LNxT[k][:, c * 512:(c + 1) * 512],
                                    start=(k == 0), stop=(k == 7)) if t else None
            r.mark('tm', f'P2mm{idx}', inst)
            r.wait('s', f'P2mm{idx}')
            r.wait('s', 'bv')
            if idx >= 2:
                r.wait('s', f'P2sp{idx - 2}')
            with LP():
                inst = sc.activation(out=stg[which][:, :], in_=psrc[:, :], func=AF.Identity,
                                     bias=bcv[:, m:m + 1]) if sc else None
            r.mark('sm', f'P2ev{idx}', inst)
            r.wait('y', f'P2ev{idx}')
            inst = y.dma_start(out=dst_d[m * 128:(m + 1) * 128, :], in_=stg[which][:, :]) if y else None
            r.mark('lspA' if which == 0 else 'lspB', f'P2sp{idx}', inst, 16)
    slice_load(0)

    for i in range(NT):
        r.wait('y', f'L2ln{i}')
        inst = y.dma_start(out=xs_d[i * 128:(i + 1) * 128, :], in_=xt(i)) if y else None
        r.mark('lxs', f'xsp{i}', inst, 16)

    if DBG == 'QT':
        r.wait('g', 'P2sp10')
        r.wait('g', 'P2sp11')
        inst = g.dma_start(out=out_d[0:1024, :],
                           in_=qt_d.rearrange("a (b c) -> (a b) c", c=1024)) if g else None
        r.mark('lst', 'dbgqt', inst, 16)
        inst = g.dma_start(out=out_d[1024:2048, :],
                           in_=kt_d.rearrange("a (b c) -> (a b) c", c=1024)) if g else None
        r.mark('lst', 'dbgkt', inst, 16)
        r.wait('g', 'dbgkt')
        return



    for h in range(H):
        QTh = QTs[h % 2][0:64, :]
        KTh = KTs[h % 2][0:64, :]
        if h > 0:
            r.wait('y', f'h{h - 1}vtp')
        inst = y.dma_start(out=VTs[0:64, :], in_=vt_d[64 * h:64 * h + 64, :]) if y else None
        r.mark('lvt', f'vts{h}', inst, 16)

        for i in range(NT):
            idx = i
            which = idx % 2
            psrc = pzBig if which == 0 else pzB2
            if idx == 0:
                r.wait('t', f'sl{h}k')
                if h == 0:
                    r.wait('t', 'P2ev10')
                    r.wait('t', 'P2ev11')
                else:
                    r.wait('t', f'h{h - 1}ubce3')
            if idx == 1 and h > 0:
                r.wait('t', f'h{h - 1}aslc')
            if idx >= 2:
                r.wait('t', f'h{h}ex{idx - 2}')
            for c in range(4):
                inst = t.matmul(psrc[:, c * 512:(c + 1) * 512],
                                QTh[:, i * 128:(i + 1) * 128],
                                KTh[:, c * 512:(c + 1) * 512],
                                start=True, stop=True) if t else None
            r.mark('tm', f'h{h}lg{idx}', inst)
            r.wait('s', f'h{h}lg{idx}')
            if h == 0 and idx == 8:
                r.wait('s', 'xsp15')
            if idx == 0 and h > 0:
                r.wait('s', f'h{h - 1}p{NS - 1}')
            with LP():
                inst = sc.activation(out=Atile[i], in_=psrc[:, :], func=AF.Exp,
                                     scale=0.125) if sc else None
            r.mark('sm', f'h{h}ex{idx}', inst)
        if h + 1 < H:
            if h >= 1:
                r.wait('y', f'h{h - 1}lg{NT - 1}')
            slice_load(h + 1)
        r.wait('t', f'h{h}ex{NT - 1}')
        r.wait('t', f'h{h}ex{NT - 2}')
        for ti in range(NT):
            for half in range(2):
                if ti > 0:
                    r.wait('t', f'h{h}bev{ti - 1}_{half}')
                elif h > 0:
                    r.wait('t', f'h{h - 1}gve{half}' if False else f'h{h - 1}aslc')
                for j in range(8 * half, 8 * half + 8):
                    inst = t.transpose(pb[half][:, (j - 8 * half) * 128:(j - 8 * half + 1) * 128],
                                       Atile[j][:, ti * 128:(ti + 1) * 128], ident[:, :]) if t else None
                r.mark('tm', f'h{h}btp{ti}_{half}', inst)
                r.wait('v', f'h{h}btp{ti}_{half}')
                with LP():
                    inst = v.tensor_copy(out=Btile[ti][:, half * 1024:(half + 1) * 1024],
                                         in_=pb[half][:, :]) if v else None
                r.mark('vm', f'h{h}bev{ti}_{half}', inst)

        inst = v.memset(gcol16[:, :], 1.0) if v else None
        r.mark('vm', f'h{h}rinit', inst)
        for s in range(NS):
            mat = Btile if s % 2 == 0 else Atile
            vecap = gcol16 if s % 2 == 0 else ucol16
            r.wait('t', f'h{h}r{s - 1}' if s > 0 else f'h{h}rinit')
            if s > 0:
                r.wait('t', f'h{h}evs{s - 1}_1')
                r.wait('t', f'h{h}evv{s - 1}_1')
            for ti in range(NT):
                if s == 0:
                    r.wait('t', f'h{h}bev{ti}_0')
                    r.wait('t', f'h{h}bev{ti}_1')
                for c in range(4):
                    inst = t.matmul(pz[c][32 * c:32 * c + 1, :], vecap[:, ti:ti + 1],
                                    mat[ti][:, c * 512:(c + 1) * 512],
                                    start=(ti == 0), stop=(ti == NT - 1),
                                    tile_position=(0, 32 * c)) if t else None
            r.mark('tm', f'h{h}p{s}', inst)
            for w in range(NWARM):
                if t:
                    t.matmul(pwarm[0:1, :], gcol16[:, 0:1],
                             Btile[0][:, (w % 4) * 512:(w % 4 + 1) * 512],
                             start=True, stop=True, tile_position=(0, 0))
            if True:
                for gq in range(4):
                    if gq < 2:
                        r.wait('t', f'h{h}evs{s}_{gq}')
                    else:
                        r.wait('t', f'h{h}evv{s}_{gq - 2}')
                    if gq == 0 and s > 0:
                        r.wait('t', f'h{h}r{s - 1}')
                    p = 32 * gq
                    for j in range(4 * gq, 4 * gq + 4):
                        inst = t.matmul(pt[:, j:j + 1],
                                        yrow[p:p + 1, j * 128:(j + 1) * 128],
                                        ones16[p:p + 1, 0:1],
                                        start=True, stop=True, tile_position=(p, 0)) if t else None
                    r.mark('tm', f'h{h}tp{s}_{gq}', inst)
            r.wait('s', f'h{h}p{s}')
            for c in range(2):
                with LP():
                    inst = sc.copy(out=yrow[32 * c:32 * c + 1, c * 512:(c + 1) * 512],
                                   in_=pz[c][32 * c:32 * c + 1, :]) if sc else None
                r.mark('sm', f'h{h}evs{s}_{c}', inst)
            r.wait('v', f'h{h}p{s}')
            for c in range(2, 4):
                with LP():
                    inst = v.tensor_copy(out=yrow[32 * c:32 * c + 1, c * 512:(c + 1) * 512],
                                         in_=pz[c][32 * c:32 * c + 1, :]) if v else None
                r.mark('vm', f'h{h}evv{s}_{c - 2}', inst)
            for gq in range(4):
                r.wait('v', f'h{h}tp{s}_{gq}')
            if s == NS - 1:
                inst = v.reciprocal(out=gcol32[:, :], in_=pt[:, :]) if v else None
            else:
                col16 = ucol16 if s % 2 == 0 else gcol16
                with LP():
                    inst = v.reciprocal(out=col16[:, :], in_=pt[:, :]) if v else None
            r.mark('vm', f'h{h}r{s}', inst)
            if s == 2:
                r.wait('t', f'vts{h}')
                if h > 0:
                    r.wait('t', f'h{h - 1}vke')
                for j in range(16):
                    inst = t.transpose(pb[0][:, j * 64:(j + 1) * 64],
                                       VTs[0:64, j * 128:(j + 1) * 128], ident[0:64, 0:64]) if t else None
                r.mark('tm', f'h{h}vtp', inst)
                r.wait('v', f'h{h}vtp')
                with LP():
                    inst = v.tensor_copy(out=Vk[:, :], in_=pb[0][:, 0:1024]) if v else None
                r.mark('vm', f'h{h}vke', inst)
            if s == NS - 1:
                r.wait('t', f'h{h}vke')
                for j in range(16):
                    dst = pb[0] if j < 8 else pb[1]
                    inst = t.transpose(dst[0:1, (j % 8) * 128:(j % 8 + 1) * 128],
                                       ucol16[:, j:j + 1], ident[:, :]) if t else None
                r.mark('tm', f'h{h}utp', inst)
                r.wait('s', f'h{h}utp')
                for half in range(2):
                    with LP():
                        inst = sc.copy(out=urow16[0:1, half * 1024:(half + 1) * 1024],
                                       in_=(pb[0] if half == 0 else pb[1])[0:1, :]) if sc else None
                r.mark('sm', f'h{h}uev', inst)
                r.wait('t', f'h{h}uev')
                r.wait('t', f'h{h}r{NS - 1}')
                r.wait('t', f'h{h}evs{NS - 1}_1')
                r.wait('t', f'h{h}evv{NS - 1}_1')
                for c in range(4):
                    inst = t.matmul(pz[c][0:64, :], ones16[0:1, 0:64],
                                    urow16[0:1, c * 512:(c + 1) * 512],
                                    start=True, stop=True, tile_position=(0, 0)) if t else None
                    r.mark('tm', f'h{h}ubc{c}', inst)
                    r.wait('v', f'h{h}ubc{c}')
                    with LP():
                        inst = v.tensor_copy(out=ubc16[0:64, c * 512:(c + 1) * 512],
                                             in_=pz[c][0:64, :]) if v else None
                    r.mark('vm', f'h{h}ubce{c}', inst)

        r.wait('v', f'h{h}r{NS - 1}')
        r.wait('v', f'h{h}vke')
        for j in range(16):
            with LP():
                inst = v.tensor_scalar_mul(gV[:, j * 64:(j + 1) * 64],
                                           Vk[:, j * 64:(j + 1) * 64],
                                           gcol32[:, j:j + 1]) if v else None
        r.mark('vm', f'h{h}gvs', inst)
        r.wait('t', f'h{h}gvs')
        r.wait('t', f'h{h}r{NS - 1}_3' if False else f'h{h}uev')
        for c in range(4):
            for ti in range(NT):
                inst = t.matmul(pzc[c][0:64, :], gV[:, ti * 64:(ti + 1) * 64],
                                Btile[ti][:, c * 512:(c + 1) * 512],
                                start=(ti == 0), stop=(ti == NT - 1)) if t else None
        r.mark('tm', f'h{h}avmm', inst)
        r.wait('v', f'h{h}avmm')
        for c in range(4):
            r.wait('v', f'h{h}ubce{c}')
        if h > 0:
            r.wait('v', f'h{h - 1}asp')
        with LP():
            inst = v.tensor_mul(aslc[0:64, :], pzB2[0:64, :], ubc16[0:64, :]) if v else None
        r.mark('vm', f'h{h}aslc', inst)
        if h < H - 1:
            r.wait('y', f'h{h}aslc')
            inst = y.dma_start(out=at_d[64 * h:64 * h + 64, :], in_=aslc[0:64, :]) if y else None
            r.mark('lat', f'h{h}asp', inst, 16)

    if DBG == 'H0':
        r.wait('g', 'h0asp')
        inst = g.dma_start(out=out_d[0:128, :],
                           in_=at_d[0:64, :].rearrange("a (b c) -> (a b) c", c=1024)) if g else None
        r.mark('lst', 'dbgh0', inst, 16)
        r.wait('g', 'dbgh0')
        return

    r.wait('g', f'h{H - 1}lg{NT - 1}')
    for j in range(4):
        inst = g.dma_start(out=Wot[j], in_=wo_d[j * 128:(j + 1) * 128, :]) if g else None
        r.mark('ldw', f'wo{j}', inst, 16)
    inst = g.dma_start(out=Woutb2[0:64, :], in_=wout_d[:, :]) if g else None
    r.mark('ldw', 'wout2', inst, 16)
    inst = g.dma_start(out=Woutb2[64:128, :], in_=wout_d[:, :]) if g else None
    r.mark('ldw', 'wout2d', inst, 16)
    for k in range(8):
        inst = g.dma_start(out=Winb2[k], in_=win_d[k * 128:(k + 1) * 128, :]) if g else None
        r.mark('ldw', f'win2_{k}', inst, 16)
    r.wait('g', f'h{H - 2}asp')
    r.wait('g', f'h{H - 1}p{NS - 1}')
    for j in range(3):
        inst = g.dma_start(out=atT[j], in_=at_d[j * 128:(j + 1) * 128, :]) if g else None
        r.mark('lata', f'atl{j}', inst, 16)
    inst = g.dma_start(out=atT[3][0:64, :], in_=at_d[384:448, :]) if g else None
    r.mark('lata', 'atl3a', inst, 16)
    for i in range(8):
        inst = g.dma_start(out=xt(i), in_=xs_d[i * 128:(i + 1) * 128, :]) if g else None
        r.mark('lrlA', f'xrl{i}', inst, 16)
    r.wait('g', f'h{H - 1}avmm')
    for i in range(8, NT):
        inst = g.dma_start(out=xt(i), in_=xs_d[i * 128:(i + 1) * 128, :]) if g else None
        r.mark('lrlB', f'xrl{i}', inst, 16)
    r.wait('g', f'h{H - 1}aslc')
    inst = g.dma_start(out=atT[3][64:128, :], in_=aslc[0:64, :]) if g else None
    r.mark('lata', 'atl3', inst, 16)

    for i in range(NT):
        which = i % 2
        if i >= 2:
            r.wait('t', f'P4xa{i - 2}')
        if i == 0:
            r.wait('t', 'atl3')
            r.wait('t', 'win2_7')
            r.wait('t', 'bo16')
            r.wait('t', f'h{H - 1}aslc')
        for nch in range(2):
            for j in range(4):
                inst = t.matmul(pzh[which][:, nch * 512:(nch + 1) * 512],
                                atT[j][:, i * 128:(i + 1) * 128],
                                Wot[j][:, nch * 512:(nch + 1) * 512],
                                start=(j == 0), stop=False) if t else None
            inst = t.matmul(pzh[which][:, nch * 512:(nch + 1) * 512],
                            ones16[0:1, 0:128], bo16[0:1, nch * 512:(nch + 1) * 512],
                            start=False, stop=True, tile_position=(0, 0)) if t else None
        r.mark('tm', f'P4mm{i}', inst)
        r.wait('v', f'P4mm{i}')
        r.wait('v', 'xrl7' if i < 8 else 'xrl15')
        inst = v.tensor_add(xt(i), xt(i), pzh[which][:, :]) if v else None
        r.mark('vm', f'P4xa{i}', inst)
        ln_stats_tile('L5', i, f'P4xa{i}')

    if DBG == 'P4':
        dbg_dump_x('P4xa')
        return

    ln_finish('L5')
    transpose_lnx('L5')
    ffn('L5', Winb2, Woutb2, 'win2_7', 'win2_7', pzfree='P4xa15')

    for i in range(NT):
        r.wait('y', f'L5xadd{i}')
        inst = y.dma_start(out=out_d[i * 128:(i + 1) * 128, :], in_=xt(i)) if y else None
        r.mark('lst', f'st{i}', inst, 16)
    r.wait('y', f'st{NT - 1}')


def build_nc():
    nc = bass.Bass("TRN2", target_bir_lowering=False, debug=False)
    names = [("x_in", [T, D], F32, "ExternalInput"),
             ("wq", [D, 512], F32, "ExternalInput"),
             ("bq", [4, 128], F32, "ExternalInput"),
             ("wk", [D, 512], F32, "ExternalInput"),
             ("bk", [4, 128], F32, "ExternalInput"),
             ("wv", [D, 512], F32, "ExternalInput"),
             ("bv", [4, 128], F32, "ExternalInput"),
             ("wo", [512, D], F32, "ExternalInput"),
             ("bo", [D], F32, "ExternalInput"),
             ("win", [D, 64], F32, "ExternalInput"),
             ("binp", [128, 1], F32, "ExternalInput"),
             ("w1", [128, 16], F32, "ExternalInput"),
             ("b1", [128, 16], F32, "ExternalInput"),
             ("w2", [128, 16], F32, "ExternalInput"),
             ("b2", [64, 1], F32, "ExternalInput"),
             ("wout", [64, D], F32, "ExternalInput"),
             ("bout", [D], F32, "ExternalInput"),
             ("identb", [128, 128], BF16, "ExternalInput"),
             ("out", [T, D], F32, "ExternalOutput"),
             ("xs", [T, D], F32, "Internal"),
             ("qts", [512, T], BF16, "Internal"),
             ("kts", [512, T], BF16, "Internal"),
             ("vts", [512, T], BF16, "Internal"),
             ("ats", [512, T], BF16, "Internal")]
    io = [nc.dram_tensor(n, s, d, kind=k).ap() for n, s, d, k in names]

    with contextlib.ExitStack() as stack:
        e = stack.enter_context
        A = e(nc.sbuf_tensor("Areg", [128, 32768], BF16))
        Bm = e(nc.sbuf_tensor("Breg", [128, 32768], BF16))
        Q = e(nc.sbuf_tensor("Qreg", [128, 30208], BF16))
        M = e(nc.sbuf_tensor("Mreg", [128, 9216], BF16))
        pzBig = e(nc.psum_tensor("pzBig", [128, 2048], F32))
        pzB2 = e(nc.psum_tensor("pzB2", [128, 2048], F32))
        sems = {nm: e(nc.semaphore(nm)) for nm in
                ('ldx', 'ldx2', 'lxs', 'ldsm', 'ldb', 'ldw', 'ldw0', 'lspA', 'lspB',
                 'lslA', 'lslB', 'lvt', 'lat', 'lata', 'lrlA', 'lrlB', 'lst',
                 'tm', 'sm', 'vm')}
        block = e(nc.Block())
        bufs = (A[:, :], Bm[:, :], Q[:, :], M[:, :], pzBig[:, :], pzB2[:, :])

        planner = Run()
        body(planner, nc, io, bufs)
        plan = planner.plan

        @block.tensor
        def _(E):
            body(Run('t', plan, sems, E), nc, io, bufs)

        @block.scalar
        def _(E):
            body(Run('s', plan, sems, E), nc, io, bufs)

        @block.vector
        def _(E):
            body(Run('v', plan, sems, E), nc, io, bufs)

        @block.sync
        def _(E):
            body(Run('y', plan, sems, E), nc, io, bufs)

        @block.gpsimd
        def _(E):
            body(Run('g', plan, sems, E), nc, io, bufs)

    return nc


def kernel(L0_attn_x, L0_attn_Wq, L0_attn_bq, L0_attn_Wk, L0_attn_bk,
           L0_attn_Wv, L0_attn_bv, L0_attn_Wo, L0_attn_bo,
           L0_ffn_Win, L0_ffn_bin_, L0_ffn_W1, L0_ffn_b1, L0_ffn_W2,
           L0_ffn_b2, L0_ffn_Wout, L0_ffn_bout):
    x = np.ascontiguousarray(np.asarray(L0_attn_x, dtype=np.float32))
    common = {
        "wq": np.ascontiguousarray(np.asarray(L0_attn_Wq, np.float32)),
        "bq": np.ascontiguousarray(np.asarray(L0_attn_bq, np.float32).reshape(4, 128)),
        "wk": np.ascontiguousarray(np.asarray(L0_attn_Wk, np.float32)),
        "bk": np.ascontiguousarray(np.asarray(L0_attn_bk, np.float32).reshape(4, 128)),
        "wv": np.ascontiguousarray(np.asarray(L0_attn_Wv, np.float32)),
        "bv": np.ascontiguousarray(np.asarray(L0_attn_bv, np.float32).reshape(4, 128)),
        "wo": np.ascontiguousarray(np.asarray(L0_attn_Wo, np.float32)),
        "bo": np.ascontiguousarray(np.asarray(L0_attn_bo, np.float32)),
        "win": np.ascontiguousarray(np.asarray(L0_ffn_Win, np.float32)),
        "binp": np.ascontiguousarray(np.tile(np.asarray(L0_ffn_bin_, np.float32).reshape(64, 1), (2, 1))),
        "w1": np.ascontiguousarray(np.tile(np.asarray(L0_ffn_W1, np.float32), (2, 1))),
        "b1": np.ascontiguousarray(np.tile(np.asarray(L0_ffn_b1, np.float32), (2, 1))),
        "w2": np.ascontiguousarray(np.tile(np.asarray(L0_ffn_W2, np.float32), (2, 1))),
        "b2": np.ascontiguousarray(np.asarray(L0_ffn_b2, np.float32).reshape(64, 1)),
        "wout": np.ascontiguousarray(np.asarray(L0_ffn_Wout, np.float32)),
        "bout": np.ascontiguousarray(np.asarray(L0_ffn_bout, np.float32)),
        "identb": np.eye(128, dtype=ml_dtypes.bfloat16),
    }
    nc = build_nc()
    in_maps = [dict(common, x_in=np.ascontiguousarray(x[b])) for b in range(8)]
    res = run_bass_kernel_spmd(nc, in_maps, core_ids=list(range(8)))
    out = np.stack([np.asarray(res.results[b]["out"]) for b in range(8)], axis=0)
    return out.astype(np.float32)
